# version 18
# baseline (speedup 1.0000x reference)
import math
from contextlib import ExitStack

import numpy as np
import ml_dtypes

import concourse.bass as bass
import concourse.mybir as mybir
from concourse.bass_core_rust import IndirectOffsetOnAxis
from concourse.bass_utils import run_bass_kernel_spmd

F32 = mybir.dt.float32
BF16 = mybir.dt.bfloat16
I32 = mybir.dt.int32
AF = mybir.ActivationFunctionType
ALU = mybir.AluOpType
AX = mybir.AxisListType
RG = [[0, 1, 2, 3, 4, 5, 6, 7]]

NCORE = 8
T = 8192
TL = T // NCORE
DM = 2048
CH = 256
NI = 1408
NSH = 2816
NE = 8
LAMBDA_INIT = 0.8 - 0.6 * math.exp(-0.3 * 12)
EPS = 1e-5

NT = T // 128
NTL = TL // 128
NDC = DM // 128
NIC = NI // 128
_BF = ml_dtypes.bfloat16

O_OFF = [0, 129, 258, 512, 641, 770, 1024, 1153]

_DEBUG_TAPS = False
_NC_CACHE = {}


def _make_consts():
    c = {}
    c["c_id32"] = np.eye(128, dtype=np.float32)
    c["c_idbf"] = np.eye(128).astype(_BF)
    nf = T // 2 + 1
    g = np.fft.irfft(np.exp(-0.5 * (np.linspace(0.0, 1.0, nf) / 0.15) ** 2), n=T)
    GB = np.zeros((128, 384))
    s = np.arange(128)[:, None]
    o = np.arange(128)[None, :]
    for off, gi in ((128, 0), (0, 1), (-128, 2)):
        d = o - s + off
        GB[:, 128 * gi:128 * (gi + 1)] = np.where(np.abs(d) <= 64, g[d % T], 0.0)
    c["c_gband"] = GB.astype(_BF)
    inv_freq = 1.0 / 10000.0 ** (np.arange(0, 64, 2, dtype=np.float64) / 64)
    fr = np.outer(np.arange(T, dtype=np.float64), inv_freq)
    c["c_cosr"] = np.cos(fr).reshape(NT, 128, 32).transpose(1, 0, 2).reshape(128, NT * 32).astype(_BF)
    c["c_sinr"] = np.sin(fr).reshape(NT, 128, 32).transpose(1, 0, 2).reshape(128, NT * 32).astype(_BF)
    kl = np.arange(128)[:, None]
    ql = np.arange(512)[None, :]
    m = np.zeros((128, 2048), dtype=np.float32)
    for r in range(4):
        m[:, 512 * r:512 * (r + 1)] = np.where(ql < kl + 128 * r, -1e30, 0.0)
    c["c_mask"] = m.astype(_BF)
    c["c_psel"] = np.ones((1, 128), dtype=np.float32)
    c["c_onebf"] = np.ones((1, 128), dtype=np.float32).astype(_BF)
    sc = np.zeros((128, 8), dtype=np.float32)
    sc[:, 0] = EPS
    sc[:, 1] = -0.5
    sc[:, 2] = 0.5
    sc[:, 3] = 1.0 / 128.0
    sc[:, 4] = math.log(0.125)
    c["c_scal"] = sc
    return c


class _Sem:
    def __init__(self, h):
        self.h = h
        self.n = 0

    def inc(self, k=1):
        self.n += k
        return self.n


class _Plan:
    ENGS = ("sy", "po", "pe", "dv", "ac")

    def __init__(self):
        self.ops = {e: [] for e in self.ENGS}
        self.marks = {}
        self.selfsems = {}

    def mark(self, name):
        self.marks[name] = {e: len(self.ops[e]) for e in self.ENGS}

    def truncate(self, name):
        if name is None or name not in self.marks:
            return
        m = self.marks[name]
        for e in self.ENGS:
            del self.ops[e][m[e]:]

    def wait(self, eng, sem, val):
        if val is None or val <= 0:
            return
        self.ops[eng].append(lambda e, s=sem.h, v=val: e.wait_ge(s, v))

    def call(self, eng, name, *args, inc=None, incv=1, dep=True, **kw):
        selfsem = self.selfsems.get(eng)
        if selfsem is not None and not dep and inc is not None:
            tok = inc.inc(incv)

            def fn(e, name=name, args=args, kw=kw, li=inc.h, lv=incv):
                getattr(e, name)(*args, **kw).then_inc(li, lv)
            self.ops[eng].append(fn)
            return tok
        if selfsem is not None:
            n_prev = selfsem.n
            if dep and n_prev > 0:
                self.ops[eng].append(
                    lambda e, s=selfsem.h, v=n_prev: e.wait_ge(s, v))
            n_now = selfsem.inc()

            def fn(e, name=name, args=args, kw=kw, s=selfsem.h):
                getattr(e, name)(*args, **kw).then_inc(s, 1)
            self.ops[eng].append(fn)
            tok = None
            if inc is not None:
                tok = inc.inc(incv)
                self.ops[eng].append(
                    lambda e, s=selfsem.h, v=n_now, li=inc.h, lv=incv:
                    (e.wait_ge(s, v), e.sem_inc(li, lv)))
            return tok

        tok = inc.inc(incv) if inc is not None else None

        def fn(e, name=name, args=args, kw=kw, inc=inc, incv=incv):
            inst = getattr(e, name)(*args, **kw)
            if inc is not None:
                inst.then_inc(inc.h, incv)
        self.ops[eng].append(fn)
        return tok

    def dma(self, eng, out, in_, sem, **kw):
        return self.call(eng, "dma_start", out=out, in_=in_, inc=sem, incv=16, **kw)


def _build(debug_taps=False, stop=None):
    nc = bass.Bass("TRN2", target_bir_lowering=False)

    def din(name, shape, dt=F32):
        return nc.dram_tensor(name, list(shape), dt, kind="ExternalInput").ap()

    x_in = din("x", (TL, DM))
    wq_in = din("wq", (DM, CH))
    wk_in = din("wk", (DM, CH))
    wv_in = din("wv", (DM, CH))
    wout_in = din("wout", (DM, DM))
    gate_in = din("gate", (NE, DM))
    lam_in = din("lamqk", (1, 256))
    rw_in = din("rmsw", (128, 1))
    w1_in = din("w1", (NE, DM, NI))
    w3_in = din("w3", (NE, DM, NI))
    w2_in = din("w2", (NE, NI, DM))
    sw1_in = din("sw1", (DM, NSH))
    sw3_in = din("sw3", (DM, NSH))
    sw2_in = din("sw2", (NSH, DM))
    gidx_in = din("gidx", (128, NT), I32)
    xidx_in = din("xidx", (128, NDC), I32)
    cid32_in = din("c_id32", (128, 128))
    cidbf_in = din("c_idbf", (128, 128), BF16)
    cg_in = din("c_gband", (128, 384), BF16)
    ccos_in = din("c_cosr", (128, NT * 32), BF16)
    csin_in = din("c_sinr", (128, NT * 32), BF16)
    cmask_in = din("c_mask", (128, 2048), BF16)
    cpsel_in = din("c_psel", (1, 128))
    conebf_in = din("c_onebf", (1, 128), BF16)
    cscal_in = din("c_scal", (128, 8))

    y_out = nc.dram_tensor("y", [TL, DM], F32, kind="ExternalOutput").ap()

    xnF_sh = nc.dram_tensor("xnF_sh", [DM, TL], BF16, kind="Internal").ap()
    xnF_all = nc.dram_tensor("xnF_all", [NCORE * DM, TL], BF16, kind="Internal").ap()
    aR_sh = nc.dram_tensor("aR_sh", [T, CH], BF16, kind="Internal").ap()
    aR_all = nc.dram_tensor("aR_all", [NCORE * T, CH], BF16, kind="Internal").ap()

    taps = {}
    if debug_taps:
        def tout(name, shape, dt):
            taps[name] = nc.dram_tensor(name, list(shape), dt, kind="ExternalOutput").ap()
        tout("t_xnF", (128, NDC * TL), BF16)
        tout("t_coeff", (1, NE * TL), BF16)
        tout("t_lam", (128, 1), F32)
        tout("t_QF", (128, 2 * T), BF16)
        tout("t_KF", (128, 2 * T), BF16)
        tout("t_V", (128, NT * 258), BF16)
        tout("t_negc", (128, 4), F32)
        tout("t_aD", (128, NT * CH), BF16)
        tout("t_outF_b", (128, NDC * TL), F32)
        tout("t_outF_d", (128, NDC * TL), F32)

    P = _Plan()
    ctx = ExitStack()
    with ctx:
        E = ctx.enter_context

        S = {}
        for sname in ["s_dvS", "s_acS",
                      "s_c", "s_xa0", "s_xa1", "s_xs0", "s_xs1", "s_wA", "s_wB",
                      "s_ws0", "s_ws1", "s_ws2", "s_ws3", "s_wg0", "s_wg1",
                      "s_wx", "s_st0", "s_st1", "s_stx", "s_cc",
                      "s_pe", "s_dv", "s_ac", "s_pe2", "s_dv2", "s_ac2"]:
            S[sname] = _Sem(E(nc.semaphore(sname)))
        s_c, s_cc = S["s_c"], S["s_cc"]
        s_xa = [S["s_xa0"], S["s_xa1"]]
        s_xs = [S["s_xs0"], S["s_xs1"]]
        s_wA, s_wB, s_wx, s_stx = S["s_wA"], S["s_wB"], S["s_wx"], S["s_stx"]
        s_ws = [S["s_ws0"], S["s_ws1"], S["s_ws2"], S["s_ws3"]]
        s_wg = [S["s_wg0"], S["s_wg1"]]
        s_st = [S["s_st0"], S["s_st1"]]
        s_pe, s_dv, s_ac = S["s_pe"], S["s_dv"], S["s_ac"]
        s_pe2, s_dv2, s_ac2 = S["s_pe2"], S["s_dv2"], S["s_ac2"]
        P.selfsems = {"dv": S["s_dvS"], "ac": S["s_acS"]}

        def barrier():
            toks = [("pe", s_pe, s_pe.n), ("pe", s_pe2, s_pe2.n),
                    ("dv", s_dv, s_dv.n), ("dv", s_dv2, s_dv2.n),
                    ("ac", s_ac, s_ac.n), ("ac", s_ac2, s_ac2.n)]
            for eng in ("pe", "dv", "ac", "sy", "po"):
                for peng, sem, tok in toks:
                    if peng != eng:
                        P.wait(eng, sem, tok)

        id32 = E(nc.sbuf_tensor("id32", [128, 128], F32))
        idbf = E(nc.sbuf_tensor("idbf", [128, 128], BF16))
        gband = E(nc.sbuf_tensor("gband", [128, 384], BF16))
        psel = E(nc.sbuf_tensor("psel", [1, 128], F32))
        onebf = E(nc.sbuf_tensor("onebf", [1, 128], BF16))
        outF = E(nc.sbuf_tensor("outF", [128, NDC * TL], F32))
        lam_col = E(nc.sbuf_tensor("lam_col", [128, 1], F32))
        rw_col = E(nc.sbuf_tensor("rw_col", [128, 1], F32))
        negc_cols = E(nc.sbuf_tensor("negc_cols", [128, 4], F32))
        smallA = E(nc.sbuf_tensor("smallA", [128, 32], F32))
        small1 = E(nc.sbuf_tensor("small1", [1, 136], F32))
        sgidx = E(nc.sbuf_tensor("sgidx", [128, NT], I32))
        sxidx = E(nc.sbuf_tensor("sxidx", [128, NDC], I32))
        cscal = E(nc.sbuf_tensor("cscal", [128, 8], F32))
        tab_scope = ExitStack()
        cosr = tab_scope.enter_context(nc.sbuf_tensor("cosr", [128, NT * 32], BF16, side="right"))
        sinr = tab_scope.enter_context(nc.sbuf_tensor("sinr", [128, NT * 32], BF16, side="right"))
        maskt = E(nc.sbuf_tensor("maskt", [128, 2048], BF16))

        tok_c = 0
        for ap_in, tile in [(cid32_in, id32), (cidbf_in, idbf), (cg_in, gband),
                            (cpsel_in, psel),
                            (conebf_in, onebf), (cscal_in, cscal),
                            (ccos_in, cosr), (csin_in, sinr), (cmask_in, maskt)]:
            tok_c = P.dma("sy", tile[:, :], ap_in, s_c)

        ab_scope = ExitStack()
        Eab = ab_scope.enter_context
        xnF_loc = Eab(nc.sbuf_tensor("xnF_loc", [128, NDC * TL], BF16))
        coeffT = Eab(nc.sbuf_tensor("coeffT", [1, NE * TL], BF16))

        with ExitStack() as pa:
            Ea = pa.enter_context
            xt = [Ea(nc.sbuf_tensor(f"xt{i}", [128, DM], F32)) for i in range(2)]
            xsq = Ea(nc.sbuf_tensor("xsq", [128, DM], F32))
            xnb = Ea(nc.sbuf_tensor("xnb", [128, DM], BF16))
            stats = Ea(nc.sbuf_tensor("stats", [128, 8], F32))
            sgw = Ea(nc.sbuf_tensor("sgw", [NE, DM], BF16))
            gwT = Ea(nc.sbuf_tensor("gwT", [128, NDC * NE], BF16))
            lgF = Ea(nc.sbuf_tensor("lgF", [NE, TL], F32))
            lgtok = Ea(nc.sbuf_tensor("lgtok", [128, NE], F32))
            gw8 = Ea(nc.sbuf_tensor("gw8", [128, NE], F32))
            slam = Ea(nc.sbuf_tensor("slam", [1, 256], F32))
            ptp = Ea(nc.psum_tensor("ptp", [128, 512], F32))
            ptpb = Ea(nc.psum_tensor("ptpb", [128, 512], BF16))
            pgl = Ea(nc.psum_tensor("pgl", [NE, TL], F32))

            P.dma("po", sgidx[:, :], gidx_in, s_wA)
            P.dma("po", sxidx[:, :], xidx_in, s_wA)
            P.dma("po", sgw[:, :], gate_in, s_wA)
            P.dma("po", slam[:, :], lam_in, s_wA)
            t_wA = P.dma("po", rw_col[:, :], rw_in, s_wA)

            P.wait("dv", s_wA, t_wA)
            P.call("dv", "tensor_mul", small1[0:1, 0:64], slam[0:1, 0:64], slam[0:1, 64:128])
            P.call("dv", "tensor_mul", small1[0:1, 64:128], slam[0:1, 128:192],
                   slam[0:1, 192:256])
            P.call("dv", "reduce_sum", small1[0:1, 128:129], small1[0:1, 0:64], axis=AX.X)
            t0 = P.call("dv", "reduce_sum", small1[0:1, 129:130], small1[0:1, 64:128],
                        axis=AX.X, inc=s_dv)
            P.wait("ac", s_c, tok_c)
            P.wait("ac", s_dv, t0)
            t1 = P.call("ac", "activation", small1[0:1, 130:132], small1[0:1, 128:130],
                        AF.Exp, inc=s_ac)
            P.wait("dv", s_ac, t1)
            P.call("dv", "tensor_sub", small1[0:1, 132:133], small1[0:1, 130:131],
                   small1[0:1, 131:132])
            t2 = P.call("dv", "tensor_scalar_add", small1[0:1, 132:133],
                        small1[0:1, 132:133], float(LAMBDA_INIT), inc=s_dv)
            P.wait("pe", s_c, tok_c)
            P.wait("pe", s_dv, t2)
            t3 = P.call("pe", "matmul", ptp[:, 0:1], psel[0:1, 0:128],
                        small1[0:1, 132:133], start=True, stop=True, inc=s_pe)
            P.wait("dv", s_pe, t3)
            P.call("dv", "tensor_copy", lam_col[:, :], ptp[:, 0:1])
            P.wait("dv", s_wA, t_wA)
            P.call("dv", "tensor_scalar_mul", rw_col[:, :], rw_col[:, :],
                   float(1.0 - LAMBDA_INIT))

            xtok = [0] * NTL
            dv_done = [0] * NTL
            pe_xnb = [0] * NTL
            for i in range(NTL):
                if i >= 2:
                    P.wait("sy", s_dv, dv_done[i - 2])
                xtok[i] = P.dma("sy", xt[i % 2][:, :], x_in[128 * i:128 * (i + 1), :],
                                s_xa[i % 2])
                P.wait("dv", s_xa[i % 2], xtok[i])
                if i >= 1:
                    P.wait("dv", s_pe, pe_xnb[i - 1])
                P.call("dv", "reduce_sum", stats[:, 0:1], xt[i % 2][:, :], axis=AX.X,
                       negate=True)
                P.call("dv", "tensor_mul", xsq[:, :], xt[i % 2][:, :], xt[i % 2][:, :])
                P.call("dv", "reduce_sum", stats[:, 1:2], xsq[:, :], axis=AX.X)
                P.call("dv", "tensor_scalar_mul", stats[:, 2:3], stats[:, 0:1], 1.0 / DM)
                P.call("dv", "tensor_mul", stats[:, 3:4], stats[:, 2:3], stats[:, 2:3])
                P.call("dv", "tensor_scalar_mul", stats[:, 4:5], stats[:, 1:2], 1.0 / DM)
                tv = P.call("dv", "tensor_sub", stats[:, 5:6], stats[:, 4:5],
                            stats[:, 3:4], inc=s_dv)
                P.wait("ac", s_dv, tv)
                P.call("ac", "activation", stats[:, 6:7], stats[:, 5:6], AF.Ln,
                       bias=cscal[:, 0:1])
                tr = P.call("ac", "activation", stats[:, 7:8], stats[:, 6:7], AF.Exp,
                            scale=cscal[:, 1:2], inc=s_ac)
                P.wait("dv", s_ac, tr)
                dv_done[i] = P.call("dv", "tensor_scalar", xnb[:, :], xt[i % 2][:, :],
                                    stats[:, 2:3], stats[:, 7:8],
                                    op0=ALU.add, op1=ALU.mult, inc=s_dv)
                P.wait("pe", s_dv, dv_done[i])
                for grp in range(4):
                    if i > 0 or grp > 0:
                        P.wait("pe", s_dv2, s_dv2.n)
                    for j in range(4):
                        jj = 4 * grp + j
                        pe_xnb[i] = P.call("pe", "transpose",
                                           ptpb[:, 128 * j:128 * (j + 1)],
                                           xnb[:, 128 * jj:128 * (jj + 1)], idbf[:, :],
                                           inc=(s_pe if j == 3 else None)) or pe_xnb[i]
                    P.wait("dv", s_pe, s_pe.n)
                    for j in range(4):
                        jj = 4 * grp + j
                        P.call("dv", "tensor_copy",
                               xnF_loc[:, TL * jj + 128 * i: TL * jj + 128 * (i + 1)],
                               ptpb[:, 128 * j:128 * (j + 1)],
                               inc=(s_dv2 if j == 3 else None))
            t_xnF = s_dv2.n

            P.wait("sy", s_dv2, t_xnF)
            t_sh = P.dma("sy", xnF_sh.rearrange("(c p) t -> p c t", p=128),
                         xnF_loc[:, :].rearrange("p (c t) -> p c t", t=TL), s_stx)
            P.wait("po", s_stx, t_sh)
            t_ag1 = P.call("po", "collective_compute", "AllGather", ALU.bypass, RG,
                           [xnF_sh.opt()], [xnF_all.opt()], inc=s_cc)

            P.wait("pe", s_wA, t_wA)
            for grp in range(4):
                P.wait("pe", s_dv2, s_dv2.n)
                for j in range(4):
                    jj = 4 * grp + j
                    P.call("pe", "transpose", ptpb[:, 128 * j:128 * j + NE],
                           sgw[:, 128 * jj:128 * (jj + 1)], idbf[0:NE, 0:NE],
                           inc=(s_pe if j == 3 else None))
                P.wait("dv", s_pe, s_pe.n)
                for j in range(4):
                    jj = 4 * grp + j
                    P.call("dv", "tensor_copy", gwT[:, NE * jj:NE * (jj + 1)],
                           ptpb[:, 128 * j:128 * j + NE],
                           inc=(s_dv2 if j == 3 else None))
            t_gwT = s_dv2.n
            P.wait("pe", s_dv2, t_gwT)
            for tb in range(2):
                for jj in range(NDC):
                    P.call("pe", "matmul", pgl[:, 512 * tb:512 * (tb + 1)],
                           gwT[:, NE * jj:NE * (jj + 1)],
                           xnF_loc[:, TL * jj + 512 * tb: TL * jj + 512 * (tb + 1)],
                           start=(jj == 0), stop=(jj == NDC - 1),
                           inc=(s_pe if (tb == 1 and jj == NDC - 1) else None))
            t_gl = s_pe.n
            P.wait("dv", s_pe, t_gl)
            t_lgF = P.call("dv", "tensor_copy", lgF[:, :], pgl[:, :], inc=s_dv)
            P.wait("pe", s_dv, t_lgF)
            for i in range(NTL):
                tk = P.call("pe", "transpose", ptp[:, 0:NE],
                            lgF[:, 128 * i:128 * (i + 1)], id32[0:NE, 0:NE], inc=s_pe)
                P.wait("dv", s_pe, tk)
                P.call("dv", "tensor_copy", lgtok[:, :], ptp[:, 0:NE])
                tm = P.call("dv", "reduce_max", smallA[:, 0:1], lgtok[:, :], axis=AX.X,
                            negate=True, inc=s_dv)
                P.wait("ac", s_dv, tm)
                te = P.call("ac", "activation", gw8[:, :], lgtok[:, :], AF.Exp,
                            bias=smallA[:, 0:1], inc=s_ac)
                P.wait("dv", s_ac, te)
                P.call("dv", "reduce_max", smallA[:, 1:2], gw8[:, :], axis=AX.X)
                P.call("dv", "tensor_scalar", smallA[:, 8:16], gw8[:, :],
                       smallA[:, 1:2], None, op0=ALU.is_ge)
                P.call("dv", "tensor_mul", smallA[:, 16:24], gw8[:, :], smallA[:, 8:16])
                P.call("dv", "tensor_sub", smallA[:, 16:24], gw8[:, :], smallA[:, 16:24])
                P.call("dv", "reduce_max", smallA[:, 2:3], smallA[:, 16:24], axis=AX.X)
                P.call("dv", "tensor_scalar", smallA[:, 8:16], gw8[:, :],
                       smallA[:, 2:3], None, op0=ALU.is_ge)
                P.call("dv", "tensor_add", smallA[:, 3:4], smallA[:, 1:2], smallA[:, 2:3])
                P.call("dv", "reciprocal", smallA[:, 4:5], smallA[:, 3:4])
                P.call("dv", "tensor_mul", smallA[:, 8:16], gw8[:, :], smallA[:, 8:16])
                tc = P.call("dv", "tensor_scalar_mul", smallA[:, 8:16], smallA[:, 8:16],
                            smallA[:, 4:5], inc=s_dv)
                P.wait("pe", s_dv, tc)
                for e in range(NE):
                    P.wait("pe", s_dv2, s_dv2.n)
                    tk = P.call("pe", "transpose", ptp[0:1, 128:256],
                                smallA[:, 8 + e:9 + e], id32[:, :], inc=s_pe)
                    P.wait("dv", s_pe, tk)
                    P.call("dv", "tensor_copy",
                           coeffT[0:1, TL * e + 128 * i: TL * e + 128 * (i + 1)],
                           ptp[0:1, 128:256], inc=s_dv2)
            t_coeff = s_dv2.n

            if debug_taps:
                P.wait("sy", s_dv2, t_coeff)
                P.dma("sy", taps["t_xnF"], xnF_loc[:, :], s_stx)
                P.dma("sy", taps["t_coeff"], coeffT[:, :], s_stx)
                P.dma("sy", taps["t_lam"], lam_col[:, :], s_stx)

        P.mark("A")
        barrier()
        P.mark("Abar")

        with ExitStack() as pb:
            Eb = pb.enter_context
            wring = Eb(nc.sbuf_tensor("wring", [128, 4 * 6144], BF16))
            m_e = Eb(nc.sbuf_tensor("m_e", [128, NIC * TL], BF16))
            ssl = Eb(nc.sbuf_tensor("ssl", [128, TL], F32))
            tg = Eb(nc.sbuf_tensor("tg", [128, TL], BF16))
            cb = Eb(nc.sbuf_tensor("cb", [128, TL], BF16))
            ph = Eb(nc.psum_tensor("ph", [128, TL], F32))
            pg = Eb(nc.psum_tensor("pg", [128, TL], F32))
            py = Eb(nc.psum_tensor("py", [128, TL], F32))
            pcb = Eb(nc.psum_tensor("pcb", [128, TL], F32))

            QW = [(0, 384), (384, 768), (768, 1152), (1152, 1408)]

            slot_free = [0, 0, 0, 0]
            t_cb = 0
            t_prev_y = 0
            for e in range(NE):
                ld = {}

                def eload(tag, slot, src, ncols):
                    P.wait("po", s_pe2, slot_free[slot])
                    dst = wring[:, 6144 * slot:6144 * slot + ncols]
                    if tag[0] in ("w1", "w3"):
                        width = ncols // NDC
                        dst = dst.rearrange("p (c i) -> p c i", i=width)
                    else:
                        dst = dst.rearrange("p (c d) -> p c d", d=DM)
                    ld[tag] = (s_ws[slot], P.dma("po", dst, src, s_ws[slot]))

                def w13src(mat, qi):
                    i0, i1 = QW[qi]
                    t = w1_in if mat == "w1" else w3_in
                    return t[e, :, i0:i1].rearrange("(c p) i -> p c i", p=128)

                for qi in (0, 1):
                    eload(("w1", qi), 2 * qi, w13src("w1", qi), NDC * (QW[qi][1] - QW[qi][0]))
                    eload(("w3", qi), 2 * qi + 1, w13src("w3", qi), NDC * (QW[qi][1] - QW[qi][0]))

                if e == 0:
                    P.wait("pe", s_c, tok_c)
                P.wait("pe", s_dv, t_cb)
                for tb in range(2):
                    P.call("pe", "matmul", pcb[:, 512 * tb:512 * (tb + 1)],
                           onebf[0:1, :],
                           coeffT[0:1, TL * e + 512 * tb: TL * e + 512 * (tb + 1)],
                           start=True, stop=True,
                           inc=(s_pe if tb == 1 else None))
                P.wait("dv", s_pe, s_pe.n)
                t_cb = P.call("dv", "tensor_copy", cb[:, :], pcb[:, :], inc=s_dv)
                if e == 0:
                    P.mark("Bcb")

                for ic in range(NIC):
                    qi = min(ic // 3, 3)
                    icl = ic - 3 * qi
                    s1, s3 = (2 * qi) % 4, (2 * qi + 1) % 4
                    qw = QW[qi][1] - QW[qi][0]
                    P.wait("pe", *ld[("w1", qi)])
                    P.wait("pe", *ld[("w3", qi)])
                    if ic >= 1 or e >= 1:
                        P.wait("pe", s_ac, s_ac.n)
                        P.wait("pe", s_dv, s_dv.n)
                        P.wait("pe", s_dv2, s_dv2.n)
                    t_h = 0
                    for (sslot, pdst, lastfl) in ((s1, ph, False), (s3, pg, True)):
                        for tb in range(2):
                            for jj in range(NDC):
                                P.call("pe", "matmul", pdst[:, 512 * tb:512 * (tb + 1)],
                                       wring[:, 6144 * sslot + qw * jj + 128 * icl:
                                             6144 * sslot + qw * jj + 128 * (icl + 1)],
                                       xnF_loc[:, TL * jj + 512 * tb: TL * jj + 512 * (tb + 1)],
                                       start=(jj == 0), stop=(jj == NDC - 1),
                                       inc=(s_pe2 if (tb == 1 and jj == NDC - 1) else None))
                        if not lastfl:
                            t_h = s_pe2.n
                    t_hg = s_pe2.n
                    slot_free[s1] = max(slot_free[s1], t_hg)
                    slot_free[s3] = max(slot_free[s3], t_hg)
                    if ic == 2:
                        eload(("w1", 2), 0, w13src("w1", 2), NDC * 384)
                        eload(("w3", 2), 1, w13src("w3", 2), NDC * 384)
                    if ic == 5:
                        eload(("w1", 3), 2, w13src("w1", 3), NDC * 256)
                        eload(("w3", 3), 3, w13src("w3", 3), NDC * 256)
                    if ic == 8:
                        P.wait("po", s_pe2, max(slot_free[0], slot_free[1]))
                        ld["w2a"] = (s_ws[0], P.dma(
                            "po",
                            wring[:, 0:6 * DM].rearrange("p (c d) -> p c d", d=DM),
                            w2_in[e, 0:768, :].rearrange("(c p) d -> p c d", p=128),
                            s_ws[0]))
                    if ic == 10:
                        P.wait("po", s_pe2, max(slot_free[2], slot_free[3]))
                        ld["w2b"] = (s_ws[2], P.dma(
                            "po",
                            wring[:, 2 * 6144:2 * 6144 + 5 * DM].rearrange(
                                "p (c d) -> p c d", d=DM),
                            w2_in[e, 768:1408, :].rearrange("(c p) d -> p c d", p=128),
                            s_ws[2]))
                    P.wait("ac", s_pe2, t_h)
                    t_sl = P.call("ac", "activation", ssl[:, :], ph[:, :], AF.Silu,
                                  inc=s_ac)
                    P.wait("dv", s_pe2, t_hg)
                    P.wait("dv", s_ac, t_sl)
                    P.wait("dv", s_dv, t_cb)
                    if e > 0 and ic == 0:
                        P.wait("dv", s_pe2, t_prev_y)
                    P.call("dv", "tensor_mul", tg[:, :], pg[:, :], cb[:, :], inc=s_dv)
                    P.call("dv", "tensor_mul", m_e[:, TL * ic:TL * (ic + 1)],
                           tg[:, :], ssl[:, :], inc=s_dv2)
                    if e == 0 and ic == 0:
                        P.mark("Bh0")
                t_m = s_dv2.n
                if e == 0:
                    P.mark("Bm0")

                P.wait("pe", s_dv2, t_m)
                ytoks = []
                for half in range(2):
                    P.wait("pe", *ld["w2a" if half == 0 else "w2b"])
                    icr = range(0, 6) if half == 0 else range(6, NIC)
                    base = 0 if half == 0 else 2 * 6144
                    for jd in range(NDC):
                        for tb in range(2):
                            slot = (2 * jd + tb) % 2
                            if len(ytoks) >= 2:
                                P.wait("pe", s_dv, ytoks[-2])
                            for ii, ic in enumerate(icr):
                                lhs = wring[:, base + DM * (ic - (0 if half == 0 else 6)) + 128 * jd:
                                            base + DM * (ic - (0 if half == 0 else 6)) + 128 * (jd + 1)]
                                P.call("pe", "matmul", py[:, 512 * slot:512 * (slot + 1)],
                                       lhs,
                                       m_e[:, TL * ic + 512 * tb: TL * ic + 512 * (tb + 1)],
                                       start=(ii == 0), stop=(ii == len(icr) - 1),
                                       inc=(s_pe2 if ii == len(icr) - 1 else None))
                            t_y = s_pe2.n
                            P.wait("dv", s_pe2, t_y)
                            dst = outF[:, TL * jd + 512 * tb: TL * jd + 512 * (tb + 1)]
                            if e == 0 and half == 0:
                                P.call("dv", "tensor_copy", dst,
                                       py[:, 512 * slot:512 * (slot + 1)], inc=s_dv)
                            else:
                                P.call("dv", "tensor_add", dst, dst,
                                       py[:, 512 * slot:512 * (slot + 1)], inc=s_dv)
                            ytoks.append(s_dv.n)
                t_prev_y = s_pe2.n
                for sl_ in range(4):
                    slot_free[sl_] = max(slot_free[sl_], t_prev_y)
                if e == 0:
                    P.mark("By0")
                if e == 1:
                    P.mark("By1")

            if debug_taps:
                P.wait("sy", s_dv, s_dv.n)
                P.dma("sy", taps["t_outF_b"], outF[:, :], s_stx)

        P.mark("B")
        ab_scope.close()
        barrier()

        with ExitStack() as pcs:
            Ec = pcs.enter_context
            QF = Ec(nc.sbuf_tensor("QF", [128, 2 * T], BF16))
            KF = Ec(nc.sbuf_tensor("KF", [128, 2 * T], BF16))
            Vt = Ec(nc.sbuf_tensor("Vt", [128, NT * 258], BF16))

            with ExitStack() as pc1:
                E1 = pc1.enter_context
                xs = E1(nc.sbuf_tensor("xs", [128, DM], BF16))
                wqk = E1(nc.sbuf_tensor("wqk", [128, NDC * 2 * CH], BF16))
                wvs = E1(nc.sbuf_tensor("wvs", [128, NDC * CH], BF16))
                qr = E1(nc.sbuf_tensor("qr", [128, 512], BF16))
                kr = E1(nc.sbuf_tensor("kr", [128, 512], BF16))
                rsc = E1(nc.sbuf_tensor("rsc", [128, 256], BF16))
                pq = E1(nc.psum_tensor("pq", [128, 1024], F32))
                pv = E1(nc.psum_tensor("pv", [128, 1024], F32))
                ptb = E1(nc.psum_tensor("ptb", [128, 512], BF16))

                P.dma("po", wqk[:, :].rearrange("p (c x h) -> p c x h", x=2, h=CH)[:, :, 0, :],
                      wq_in.rearrange("(c p) h -> p c h", p=128), s_wB)
                P.dma("po", wqk[:, :].rearrange("p (c x h) -> p c x h", x=2, h=CH)[:, :, 1, :],
                      wk_in.rearrange("(c p) h -> p c h", p=128), s_wB)
                twqkv = P.dma("po", wvs[:, :].rearrange("p (c h) -> p c h", h=CH),
                              wv_in.rearrange("(c p) h -> p c h", p=128), s_wB)
                P.wait("dv", s_wB, twqkv)
                P.wait("dv", s_c, tok_c)
                t_wqsc = P.call("dv", "tensor_scalar_mul",
                                wqk[:, :].rearrange("p (c x h) -> p c x h", x=2, h=CH)[:, :, 0, :],
                                wqk[:, :].rearrange("p (c x h) -> p c x h", x=2, h=CH)[:, :, 0, :],
                                float(64 ** -0.5), inc=s_dv)
                t_vmemset = P.call("dv", "memset", Vt[:, :], 1.0, inc=s_dv)

                xs_tokA = [0] * NT
                xs_tokB = [0] * NT
                mid_tok = [0] * NT
                qkv_tok = [0] * NT
                rope_tok = [0] * NT
                tp_tok = [0] * NT
                act_tok = [0] * (NT + 1)
                P.wait("sy", s_cc, t_ag1)
                for it in range(NT + 1):
                    i = it
                    j = it - 1
                    if i < NT:
                        r, il = i // NTL, i % NTL
                        if i >= 1:
                            P.wait("sy", s_pe2, mid_tok[i - 1])
                        xs_tokA[i] = P.dma(
                            "sy", xs[:, 0:1024].rearrange("p (c t) -> p c t", t=128),
                            xnF_all[DM * r:DM * r + 1024, 128 * il:128 * (il + 1)]
                            .rearrange("(c p) t -> p c t", p=128), s_xs[0])
                        if i >= 1:
                            P.wait("sy", s_pe2, qkv_tok[i - 1])
                        xs_tokB[i] = P.dma(
                            "sy", xs[:, 1024:2048].rearrange("p (c t) -> p c t", t=128),
                            xnF_all[DM * r + 1024:DM * (r + 1), 128 * il:128 * (il + 1)]
                            .rearrange("(c p) t -> p c t", p=128), s_xs[1])
                        P.wait("pe", s_xs[0], xs_tokA[i])
                        if i == 0:
                            P.wait("pe", s_wB, twqkv)
                            P.wait("pe", s_dv, t_wqsc)
                        if i >= 2:
                            P.wait("pe", s_dv, rope_tok[i - 2])
                            P.wait("pe", s_ac2, act_tok[i - 2])
                        sl = 512 * (i % 2)
                        for jj in range(NDC):
                            if jj == 8:
                                P.wait("pe", s_xs[1], xs_tokB[i])
                            lhs = xs[:, 128 * jj:128 * (jj + 1)]
                            st_, sp_ = (jj == 0), (jj == NDC - 1)
                            P.call("pe", "matmul", pq[:, sl:sl + 512], lhs,
                                   wqk[:, 512 * jj:512 * (jj + 1)], start=st_, stop=sp_)
                            mm_inc = s_pe2 if (jj == 7 or sp_) else None
                            P.call("pe", "matmul", pv[:, sl:sl + 256], lhs,
                                   wvs[:, CH * jj:CH * (jj + 1)], start=st_, stop=sp_,
                                   inc=mm_inc)
                            if jj == 7:
                                mid_tok[i] = s_pe2.n
                        qkv_tok[i] = s_pe2.n

                    if j >= 0:
                        qsl = 256 * (j % 2)
                        P.wait("pe", s_dv, rope_tok[j])
                        if j >= 1:
                            P.wait("pe", s_ac2, act_tok[j])
                        for half, ss in enumerate((qr, qr, kr, kr)):
                            off = qsl + 128 * (half % 2)
                            P.call("pe", "transpose",
                                   ptb[:, 128 * half:128 * (half + 1)],
                                   ss[:, off:off + 128], idbf[:, :],
                                   inc=(s_pe if half == 3 else None))
                        tp_tok[j] = s_pe.n

                    if i < NT:
                        sl = 512 * (i % 2)
                        P.wait("ac", s_pe2, qkv_tok[i])
                        if i == 0:
                            P.wait("ac", s_dv, t_vmemset)
                        vinc = s_ac2 if j < 0 else None
                        P.call("ac", "copy", Vt[:, 258 * i:258 * i + 128],
                               pv[:, sl:sl + 128], dep=False, inc=vinc)
                        P.call("ac", "copy", Vt[:, 258 * i + 129:258 * i + 257],
                               pv[:, sl + 128:sl + 256], dep=False, inc=vinc)
                    if j >= 0:
                        P.wait("ac", s_pe, tp_tok[j])
                        for half in range(4):
                            dstT = (QF if half < 2 else KF)
                            hc = half % 2
                            P.call("ac", "copy",
                                   dstT[:, T * hc + 128 * j: T * hc + 128 * (j + 1)],
                                   ptb[:, 128 * half:128 * (half + 1)], dep=False,
                                   inc=(s_ac2 if half == 3 else None))
                    act_tok[it] = s_ac2.n

                    if i < NT:
                        sl = 512 * (i % 2)
                        qsl = 256 * (i % 2)
                        P.wait("dv", s_pe2, qkv_tok[i])
                        if i >= 2:
                            P.wait("dv", s_pe, tp_tok[i - 2])
                        cosb = cosr[:, 32 * i:32 * (i + 1)].rearrange(
                            "p (o j) -> p o j", o=1).broadcast_to([128, 4, 32])
                        sinb = sinr[:, 32 * i:32 * (i + 1)].rearrange(
                            "p (o j) -> p o j", o=1).broadcast_to([128, 4, 32])
                        for (srcoff, dst) in ((0, qr), (256, kr)):
                            sv = pq[:, sl + srcoff:sl + srcoff + 256].rearrange(
                                "p (s j) -> p s j", s=4)
                            dvw = dst[:, qsl:qsl + 256].rearrange(
                                "p (s j) -> p s j", s=4)
                            ta = rsc[:, 0:128].rearrange("p (s j) -> p s j", s=4)
                            tb_ = rsc[:, 128:256].rearrange("p (s j) -> p s j", s=4)
                            ev = sv[:, :, 0:64:2]
                            od = sv[:, :, 1:64:2]
                            P.call("dv", "tensor_mul", ta, ev, cosb)
                            P.call("dv", "tensor_mul", tb_, od, sinb, dep=False)
                            P.call("dv", "tensor_sub", dvw[:, :, 0:64:2], ta, tb_)
                            P.call("dv", "tensor_mul", ta, ev, sinb)
                            P.call("dv", "tensor_mul", tb_, od, cosb, dep=False)
                            P.call("dv", "tensor_add", dvw[:, :, 1:64:2], ta, tb_,
                                   inc=(s_dv if dst is kr else None))
                        rope_tok[i] = s_dv.n

            with ExitStack() as pcb_s:
                Ecb = pcb_s.enter_context
                s2scr = Ecb(nc.sbuf_tensor("s2scr", [128, 512], BF16))
                nsq = Ecb(nc.sbuf_tensor("nsq", [1, 16], F32))
                pns = Ecb(nc.psum_tensor("pns", [1, 1024], F32))
                pbc = Ecb(nc.psum_tensor("pbc", [128, 4], F32))

                t_qkv_evac = act_tok[NT]
                P.wait("ac", s_ac2, t_qkv_evac)
                P.wait("pe", s_ac2, t_qkv_evac)
                prev_mm = 0
                for mi, MF in ((0, QF), (1, KF)):
                    for slc in range(32):
                        P.wait("ac", s_pe2, prev_mm)
                        tq = P.call("ac", "activation", s2scr[:, :],
                                    MF[:, 512 * slc:512 * (slc + 1)], AF.Square,
                                    inc=s_ac2)
                        P.wait("pe", s_ac2, tq)
                        slot = slc % 2
                        P.wait("pe", s_dv, s_dv.n)
                        prev_mm = P.call("pe", "matmul",
                                         pns[:, 512 * slot:512 * (slot + 1)],
                                         Vt[:, 128:129], s2scr[:, :],
                                         start=True, stop=True, inc=s_pe2)
                        P.wait("dv", s_pe2, prev_mm)
                        col = 2 * mi + slc // 16
                        P.call("dv", "reduce_max", nsq[0:1, 8 + col:9 + col],
                               pns[0:1, 512 * slot:512 * (slot + 1)], axis=AX.X)
                        if slc % 16 == 0:
                            P.call("dv", "tensor_copy", nsq[0:1, col:col + 1],
                                   nsq[0:1, 8 + col:9 + col], inc=s_dv)
                        else:
                            P.call("dv", "tensor_max", nsq[0:1, col:col + 1],
                                   nsq[0:1, col:col + 1], nsq[0:1, 8 + col:9 + col],
                                   inc=s_dv)
                t_ns = P.call("dv", "tensor_mul", nsq[0:1, 4:6], nsq[0:1, 0:2],
                              nsq[0:1, 2:4], inc=s_dv)
                P.wait("ac", s_dv, t_ns)
                P.call("ac", "activation", nsq[0:1, 6:8], nsq[0:1, 4:6], AF.Ln)
                t_c = P.call("ac", "activation", nsq[0:1, 6:8], nsq[0:1, 6:8],
                             AF.Exp, scale=cscal[0:1, 2:3], inc=s_ac)
                P.wait("dv", s_ac, t_c)
                t_negc = P.call("dv", "tensor_scalar_mul", nsq[0:1, 6:8],
                                nsq[0:1, 6:8], -1.0, inc=s_dv)
                P.wait("pe", s_dv, t_negc)
                for h in range(2):
                    P.call("pe", "matmul", pbc[:, h:h + 1],
                           psel[0:1, 0:128], nsq[0:1, 6 + h:7 + h],
                           start=True, stop=True,
                           inc=(s_pe if h == 1 else None))
                t_bc = s_pe.n
                P.wait("dv", s_pe, t_bc)
                t_negc2 = P.call("dv", "tensor_copy", negc_cols[:, 0:2],
                                 pbc[:, 0:2], inc=s_dv)
                if debug_taps:
                    P.wait("sy", s_dv, t_negc2)
                    P.dma("sy", taps["t_QF"], QF[:, :], s_stx)
                    P.dma("sy", taps["t_KF"], KF[:, :], s_stx)
                    P.dma("sy", taps["t_V"], Vt[:, :], s_stx)
                    P.dma("sy", taps["t_negc"], negc_cols[:, :], s_stx)

            P.mark("C1")
            barrier()
            tab_scope.close()

            aD = Ec(nc.sbuf_tensor("aD", [128, NT * CH], BF16))
            with ExitStack() as pc2:
                E2 = pc2.enter_context
                at_all = E2(nc.sbuf_tensor("at_all", [128, 3072], BF16))
                sc1 = E2(nc.sbuf_tensor("sc1", [128, 128], BF16))
                sc2 = E2(nc.sbuf_tensor("sc2", [128, 128], BF16))
                rsum = E2(nc.sbuf_tensor("rsum", [128, 8], F32))
                ps_all = E2(nc.psum_tensor("ps_all", [128, 2048], F32))
                po = E2(nc.psum_tensor("po", [128, 1536], F32))

                t_oevac = 0
                exp_seq = []
                s_seq = []
                av_seq = []

                def emit_S(h, qb, kc, g):
                    slot = kc % 2
                    if g >= 2:
                        P.wait("pe", s_ac, exp_seq[g - 2])
                    diag = kc - 4 * qb
                    on_diag = 0 <= diag <= 3
                    for p in range(2):
                        P.call("pe", "matmul",
                               ps_all[:, 1024 * slot + 512 * p:1024 * slot + 512 * (p + 1)],
                               KF[64 * p:64 * (p + 1),
                                  T * h + 128 * kc:128 * (kc + 1) + T * h],
                               QF[64 * p:64 * (p + 1),
                                  T * h + 512 * qb:512 * (qb + 1) + T * h],
                               start=True, stop=not on_diag, tile_position=(64 * p, 0),
                               inc=(s_pe2 if (p == 1 and not on_diag) else None))
                    if on_diag:
                        for p in range(2):
                            base = 1024 * slot + 512 * p
                            if diag > 0:
                                P.call("pe", "matmul",
                                       ps_all[:, base:base + 128 * diag],
                                       idbf[:, :], maskt[:, 1536:1536 + 128 * diag],
                                       start=False, stop=False,
                                       skip_group_check=True)
                            P.call("pe", "matmul",
                                   ps_all[:, base + 128 * diag:base + 128 * (diag + 1)],
                                   idbf[:, :], maskt[:, 0:128],
                                   start=False, stop=True, skip_group_check=True,
                                   inc=(s_pe2 if p == 1 else None))
                    s_seq.append(s_pe2.n)

                def emit_exp(h, kc, g):
                    slot = kc % 2
                    aslot = g % 3
                    P.wait("ac", s_pe2, s_seq[g])
                    if g >= 3:
                        P.wait("ac", s_pe, av_seq[g - 3])
                    P.call("ac", "activation", at_all[:, 1024 * aslot:1024 * (aslot + 1)],
                           ps_all[:, 1024 * slot:1024 * (slot + 1)], AF.Exp,
                           bias=negc_cols[:, h:h + 1], inc=s_ac, dep=False)
                    exp_seq.append(s_ac.n)

                def emit_AV(h, kc, g, first):
                    aslot = g % 3
                    P.wait("pe", s_ac, exp_seq[g])
                    for p in range(2):
                        for qt in range(4):
                            u = 4 * p + qt
                            P.call("pe", "matmul",
                                   po[:, O_OFF[u]:O_OFF[u] + 129],
                                   at_all[:, 1024 * aslot + 512 * p + 128 * qt:
                                          1024 * aslot + 512 * p + 128 * (qt + 1)],
                                   Vt[:, 258 * kc + 129 * h:258 * kc + 129 * (h + 1)],
                                   start=(first and u in (0, 3, 6)), stop=False,
                                   skip_group_check=True,
                                   inc=(s_pe if (p == 1 and qt == 3) else None))
                    av_seq.append(s_pe.n)

                g0 = 0
                for h in range(2):
                    for qb in range(16):
                        nkc = 4 * qb + 4
                        emit_S(h, qb, 0, g0 + 0)
                        emit_exp(h, 0, g0 + 0)
                        emit_S(h, qb, 1, g0 + 1)
                        emit_exp(h, 1, g0 + 1)
                        for kc in range(nkc):
                            g = g0 + kc
                            if kc + 2 < nkc:
                                emit_S(h, qb, kc + 2, g + 2)
                                emit_exp(h, kc + 2, g + 2)
                            if kc == 0:
                                P.wait("pe", s_dv, t_oevac)
                            emit_AV(h, kc, g, kc == 0)
                        P.wait("dv", s_pe, av_seq[-1])
                        for u in range(8):
                            P.call("dv", "reciprocal", rsum[:, u:u + 1],
                                   po[:, O_OFF[u] + 128:O_OFF[u] + 129], dep=False)
                        for qt in range(4):
                            u1, u2 = qt, 4 + qt
                            P.call("dv", "tensor_scalar_mul", sc1[:, :],
                                   po[:, O_OFF[u1]:O_OFF[u1] + 128],
                                   rsum[:, u1:u1 + 1])
                            P.call("dv", "tensor_scalar", sc2[:, :],
                                   po[:, O_OFF[u2]:O_OFF[u2] + 128],
                                   rsum[:, u2:u2 + 1], lam_col[:, 0:1],
                                   op0=ALU.mult, op1=ALU.mult, dep=False)
                            tile = 4 * qb + qt
                            P.call("dv", "tensor_sub",
                                   aD[:, 256 * tile + 128 * h:256 * tile + 128 * (h + 1)],
                                   sc1[:, :], sc2[:, :],
                                   inc=(s_dv if qt == 3 else None))
                        t_oevac = s_dv.n
                        g0 += nkc
                t_aD = s_dv.n
                P.mark("C2")
                if debug_taps:
                    P.wait("sy", s_dv, t_aD)
                    P.dma("sy", taps["t_aD"], aD[:, :], s_stx)

            with ExitStack() as pc3:
                E3 = pc3.enter_context
                aRst = E3(nc.sbuf_tensor("aRst", [128, 512], BF16))
                sqsc = E3(nc.sbuf_tensor("sqsc", [128, 128], F32))
                ssq = E3(nc.sbuf_tensor("ssq", [128, 4], F32))
                pcv = E3(nc.psum_tensor("pcv", [128, 1024], F32))

                st_tok = [0] * NT
                rms_tok = [0] * NT
                ar_tok = [0] * NT
                P.wait("pe", s_dv, t_aD)
                for k in range(NT):
                    slot = k % 2
                    psl = 512 * slot
                    if k >= 2:
                        P.wait("pe", s_ac, rms_tok[k - 2])
                        P.wait("pe", s_dv, ar_tok[k - 2])
                    for gi2, srck in enumerate(((k - 1) % NT, k, (k + 1) % NT)):
                        P.call("pe", "matmul", pcv[:, psl:psl + 256],
                               gband[:, 128 * gi2:128 * (gi2 + 1)],
                               aD[:, 256 * srck:256 * (srck + 1)],
                               start=(gi2 == 0), stop=(gi2 == 2),
                               inc=(s_pe2 if gi2 == 2 else None))
                    t_cv = s_pe2.n
                    P.wait("ac", s_pe2, t_cv)
                    P.call("ac", "activation", sqsc[:, :],
                           pcv[:, psl:psl + 128], AF.Square,
                           accum_out=ssq[:, 0:1])
                    P.call("ac", "activation", sqsc[:, :],
                           pcv[:, psl + 128:psl + 256], AF.Square,
                           accum_out=ssq[:, 1:2])
                    P.call("ac", "activation", ssq[:, 2:4], ssq[:, 0:2], AF.Ln,
                           scale=cscal[:, 3:4], bias=cscal[:, 0:1])
                    rms_tok[k] = P.call("ac", "activation", ssq[:, 2:4], ssq[:, 2:4],
                                        AF.Exp, scale=cscal[:, 1:2], inc=s_ac)
                    P.wait("dv", s_ac, rms_tok[k])
                    if k >= 2:
                        P.wait("dv", s_st[k % 2], st_tok[k - 2])
                    for h in range(2):
                        P.call("dv", "tensor_scalar_mul",
                               aRst[:, 256 * slot + 128 * h:256 * slot + 128 * (h + 1)],
                               pcv[:, psl + 128 * h:psl + 128 * (h + 1)],
                               ssq[:, 2 + h:3 + h],
                               inc=(s_dv if h == 1 else None))
                    ar_tok[k] = s_dv.n
                    P.wait("sy", s_dv, ar_tok[k])
                    st_tok[k] = P.dma("sy", aR_sh[128 * k:128 * (k + 1), :],
                                      aRst[:, 256 * slot:256 * (slot + 1)], s_st[k % 2])
                P.wait("po", s_st[0], s_st[0].n)
                P.wait("po", s_st[1], s_st[1].n)
                t_ag2 = P.call("po", "collective_compute", "AllGather", ALU.bypass, RG,
                               [aR_sh.opt()], [aR_all.opt()], inc=s_cc)

        P.mark("C3")
        barrier()

        with ExitStack() as pd:
            Ed = pd.enter_context
            xnr = Ed(nc.sbuf_tensor("xnr", [128, NDC * TL], BF16))
            swr = Ed(nc.sbuf_tensor("swr", [128, 4 * 6144], BF16))
            ssl2 = Ed(nc.sbuf_tensor("ssl2", [128, TL], F32))
            msh = Ed(nc.sbuf_tensor("msh", [128, 22 * TL], BF16))
            ph2 = Ed(nc.psum_tensor("ph2", [128, TL], F32))
            pg2 = Ed(nc.psum_tensor("pg2", [128, TL], F32))
            py2 = Ed(nc.psum_tensor("py2", [128, TL], F32))

            P.wait("po", s_wA, t_wA)
            P.wait("po", s_cc, t_ag1)
            t_xnr = 0
            for jj in range(NDC):
                t_xnr = P.call("po", "indirect_dma_start",
                               xnr[:, TL * jj:TL * (jj + 1)], None, xnF_all,
                               IndirectOffsetOnAxis(sxidx[:, jj:jj + 1], 0),
                               inc=s_wx, incv=16)

            SWQ = [(384 * i, min(384 * (i + 1), NSH)) for i in range(8)]
            sl_free = [0, 0, 0, 0]
            ld1 = {}

            def eload2(tag, slot, mat, si):
                P.wait("po", s_pe2, sl_free[slot])
                i0, i1 = SWQ[si]
                w = i1 - i0
                t = sw1_in if mat == "sw1" else sw3_in
                dst = swr[:, 6144 * slot:6144 * slot + NDC * w].rearrange(
                    "p (c i) -> p c i", i=w)
                ld1[tag] = (s_ws[slot], P.dma(
                    "po", dst, t[:, i0:i1].rearrange("(c p) i -> p c i", p=128),
                    s_ws[slot]))

            for si in (0, 1):
                eload2(("sw1", si), 2 * si, "sw1", si)
                eload2(("sw3", si), 2 * si + 1, "sw3", si)

            P.wait("pe", s_wx, t_xnr)
            for ic in range(22):
                si = ic // 3
                icl = ic - 3 * si
                s1, s3 = (2 * si) % 4, (2 * si + 1) % 4
                qw = SWQ[si][1] - SWQ[si][0]
                P.wait("pe", *ld1[("sw1", si)])
                P.wait("pe", *ld1[("sw3", si)])
                if ic >= 1:
                    P.wait("pe", s_ac, s_ac.n)
                    P.wait("pe", s_dv2, s_dv2.n)
                t_h2 = 0
                for (sslot, pdst, lastfl) in ((s1, ph2, False), (s3, pg2, True)):
                    for tb in range(2):
                        for jj in range(NDC):
                            P.call("pe", "matmul", pdst[:, 512 * tb:512 * (tb + 1)],
                                   swr[:, 6144 * sslot + qw * jj + 128 * icl:
                                       6144 * sslot + qw * jj + 128 * (icl + 1)],
                                   xnr[:, TL * jj + 512 * tb: TL * jj + 512 * (tb + 1)],
                                   start=(jj == 0), stop=(jj == NDC - 1),
                                   inc=(s_pe2 if (tb == 1 and jj == NDC - 1) else None))
                    if not lastfl:
                        t_h2 = s_pe2.n
                t_hg = s_pe2.n
                sl_free[s1] = max(sl_free[s1], t_hg)
                sl_free[s3] = max(sl_free[s3], t_hg)
                last_of_si = (icl == 2) or (ic == 21)
                if last_of_si and si + 2 < 8:
                    eload2(("sw1", si + 2), (2 * (si + 2)) % 4, "sw1", si + 2)
                    eload2(("sw3", si + 2), (2 * (si + 2) + 1) % 4, "sw3", si + 2)
                P.wait("ac", s_pe2, t_h2)
                t_sl = P.call("ac", "activation", ssl2[:, :], ph2[:, :], AF.Silu,
                              inc=s_ac)
                P.wait("dv", s_pe2, t_hg)
                P.wait("dv", s_ac, t_sl)
                P.call("dv", "tensor_mul", msh[:, TL * ic:TL * (ic + 1)],
                       pg2[:, :], ssl2[:, :], inc=s_dv2)
            t_msh = s_dv2.n

            W2P = [(0, 768), (768, 1536), (1536, 2304), (2304, 2816)]
            tw2 = [0] * 4
            pslots = [(0, 1), (2, 3), (0, 1), (2, 3)]

            def load_sw2(k):
                r0, r1 = W2P[k]
                a, bsl = pslots[k]
                gate = max(sl_free[a], sl_free[bsl])
                P.wait("po", s_pe2, gate)
                nch = (r1 - r0) // 128
                tw2[k] = P.dma("po",
                               swr[:, 6144 * a:6144 * a + nch * DM].rearrange(
                                   "p (c d) -> p c d", d=DM),
                               sw2_in[r0:r1, :].rearrange("(c p) d -> p c d", p=128),
                               s_ws[a])

            load_sw2(0)
            ytoks2 = []
            for k in range(4):
                a, bsl = pslots[k]
                r0, r1 = W2P[k]
                nch = (r1 - r0) // 128
                P.wait("pe", s_ws[a], tw2[k])
                if k == 0:
                    P.wait("pe", s_dv2, t_msh)
                for jd in range(NDC):
                    for tb in range(2):
                        slot = (2 * jd + tb) % 2
                        if len(ytoks2) >= 2:
                            P.wait("pe", s_dv, ytoks2[-2])
                        for icl in range(nch):
                            ic = r0 // 128 + icl
                            P.call("pe", "matmul", py2[:, 512 * slot:512 * (slot + 1)],
                                   swr[:, 6144 * a + DM * icl:6144 * a + DM * icl + DM][:, 128 * jd:128 * (jd + 1)],
                                   msh[:, TL * ic + 512 * tb: TL * ic + 512 * (tb + 1)],
                                   start=(icl == 0), stop=(icl == nch - 1),
                                   inc=(s_pe2 if icl == nch - 1 else None))
                        t_y = s_pe2.n
                        P.wait("dv", s_pe2, t_y)
                        dst = outF[:, TL * jd + 512 * tb: TL * jd + 512 * (tb + 1)]
                        P.call("dv", "tensor_add", dst, dst,
                               py2[:, 512 * slot:512 * (slot + 1)], inc=s_dv)
                        ytoks2.append(s_dv.n)
                for s_ in pslots[k]:
                    sl_free[s_] = s_pe2.n
                if k + 1 < 4:
                    load_sw2(k + 1)
            if debug_taps:
                P.wait("sy", s_dv, s_dv.n)
                P.dma("sy", taps["t_outF_d"], outF[:, :], s_stx)

        P.mark("D")
        barrier()

        with ExitStack() as pe_s:
            Ee = pe_s.enter_context
            wob = [Ee(nc.sbuf_tensor(f"wob{i}", [128, 4 * DM], BF16)) for i in range(2)]
            aRF = Ee(nc.sbuf_tensor("aRF", [128, NDC * TL], BF16))
            artok = Ee(nc.sbuf_tensor("artok", [128, 2 * CH], BF16))
            xrl = [Ee(nc.sbuf_tensor(f"xrl{i}", [128, DM], F32)) for i in range(2)]
            outst = [Ee(nc.sbuf_tensor(f"outst{i}", [128, DM], F32)) for i in range(2)]
            ptb2 = Ee(nc.psum_tensor("ptb2", [128, 512], BF16))
            ppr = Ee(nc.psum_tensor("ppr", [128, 1024], F32))
            ptf = Ee(nc.psum_tensor("ptf", [128, 512], F32))

            P.wait("po", s_cc, t_ag2)
            g_tok = [0] * NT
            tp_tok = [0] * NT
            ev_tok = [0] * NT
            for k in range(NT):
                slot = k % 2
                if k >= 2:
                    P.wait("po", s_pe2, tp_tok[k - 2])
                g_tok[k] = P.call("po", "indirect_dma_start",
                                  artok[:, CH * slot:CH * (slot + 1)], None, aR_all,
                                  IndirectOffsetOnAxis(sgidx[:, k:k + 1], 0),
                                  inc=s_wg[slot], incv=16)
                P.wait("pe", s_wg[slot], g_tok[k])
                if k >= 1:
                    P.wait("pe", s_dv2, ev_tok[k - 1])
                for half in range(2):
                    P.call("pe", "transpose", ptb2[:, 128 * half:128 * (half + 1)],
                           artok[:, CH * slot + 128 * half:CH * slot + 128 * (half + 1)],
                           idbf[:, :], inc=(s_pe2 if half == 1 else None))
                tp_tok[k] = s_pe2.n
                P.wait("dv", s_pe2, tp_tok[k])
                r, i = k // NTL, k % NTL
                for half in range(2):
                    cc2 = 2 * r + half
                    P.call("dv", "tensor_copy",
                           aRF[:, TL * cc2 + 128 * i: TL * cc2 + 128 * (i + 1)],
                           ptb2[:, 128 * half:128 * (half + 1)],
                           inc=(s_dv2 if half == 1 else None))
                ev_tok[k] = s_dv2.n
            t_aRF = s_dv2.n

            wo_use = [0, 0, 0, 0]
            wo_tok = [0, 0, 0, 0]
            wo_sc = [0, 0, 0, 0]

            def load_wo(q):
                if q >= 2:
                    P.wait("po", s_pe2, wo_use[q - 2])
                wo_tok[q] = P.dma("po",
                                  wob[q % 2][:, :].rearrange("p (c d) -> p c d", d=DM),
                                  wout_in[512 * q:512 * (q + 1), :]
                                  .rearrange("(c p) d -> p c d", p=128), s_ws[q % 2])

            load_wo(0)
            load_wo(1)
            P.wait("pe", s_dv2, t_aRF)
            for q in range(4):
                P.wait("dv", s_ws[q % 2], wo_tok[q])
                if q >= 2:
                    P.wait("dv", s_pe2, wo_use[q - 2])
                wo_sc[q] = P.call("dv", "tensor_scalar_mul", wob[q % 2][:, :],
                                  wob[q % 2][:, :], rw_col[:, 0:1], inc=s_dv2)
                P.wait("pe", s_dv2, wo_sc[q])
                if q == 0:
                    ytoks3 = []
                for jd in range(NDC):
                    for tb in range(2):
                        slot = (2 * jd + tb) % 2
                        if len(ytoks3) >= 2:
                            P.wait("pe", s_dv, ytoks3[-2])
                        for ccl in range(4):
                            cc2 = 4 * q + ccl
                            P.call("pe", "matmul", ppr[:, 512 * slot:512 * (slot + 1)],
                                   wob[q % 2][:, DM * ccl + 128 * jd: DM * ccl + 128 * (jd + 1)],
                                   aRF[:, TL * cc2 + 512 * tb: TL * cc2 + 512 * (tb + 1)],
                                   start=(ccl == 0), stop=(ccl == 3),
                                   inc=(s_pe2 if ccl == 3 else None))
                        t_y = s_pe2.n
                        P.wait("dv", s_pe2, t_y)
                        dst = outF[:, TL * jd + 512 * tb: TL * jd + 512 * (tb + 1)]
                        P.call("dv", "tensor_add", dst, dst,
                               ppr[:, 512 * slot:512 * (slot + 1)], inc=s_dv)
                        ytoks3.append(s_dv.n)
                wo_use[q] = s_pe2.n
                if q + 2 < 4:
                    load_wo(q + 2)
            t_proj = s_dv.n

            xr_tok = [0] * NTL
            fin_tok = [0] * NTL
            out_st = [0] * NTL
            P.wait("pe", s_dv, t_proj)
            for it in range(NTL):
                if it >= 2:
                    P.wait("sy", s_dv, fin_tok[it - 2])
                xr_tok[it] = P.dma("sy", xrl[it % 2][:, :],
                                   x_in[128 * it:128 * (it + 1), :], s_xa[it % 2])
                for grp in range(4):
                    if it > 0 or grp > 0:
                        P.wait("pe", s_dv, s_dv.n)
                    for j in range(4):
                        jj = 4 * grp + j
                        P.call("pe", "transpose", ptf[:, 128 * j:128 * (j + 1)],
                               outF[:, TL * jj + 128 * it: TL * jj + 128 * (it + 1)],
                               id32[:, :], inc=(s_pe2 if j == 3 else None))
                    t_tf = s_pe2.n
                    P.wait("dv", s_pe2, t_tf)
                    P.wait("dv", s_xa[it % 2], xr_tok[it])
                    if it >= 2 and grp == 0:
                        P.wait("dv", s_st[it % 2], out_st[it - 2])
                    for j in range(4):
                        jj = 4 * grp + j
                        P.call("dv", "tensor_add",
                               outst[it % 2][:, 128 * jj:128 * (jj + 1)],
                               ptf[:, 128 * j:128 * (j + 1)],
                               xrl[it % 2][:, 128 * jj:128 * (jj + 1)],
                               inc=(s_dv if j == 3 else None))
                fin_tok[it] = s_dv.n
                P.wait("sy", s_dv, fin_tok[it])
                out_st[it] = P.dma("sy", y_out[128 * it:128 * (it + 1), :],
                                   outst[it % 2][:, :], s_st[it % 2])
            P.wait("sy", s_st[0], s_st[0].n)
            P.wait("sy", s_st[1], s_st[1].n)
            P.wait("sy", s_stx, s_stx.n)

        P.mark("full")
        P.truncate(stop)

        block = E(nc.Block())

        @block.sync
        def _(sync):
            for f in P.ops["sy"]:
                f(sync)

        @block.gpsimd
        def _(g):
            for f in P.ops["po"]:
                f(g)

        @block.tensor
        def _(t):
            for f in P.ops["pe"]:
                f(t)

        @block.vector
        def _(v):
            for f in P.ops["dv"]:
                f(v)

        @block.scalar
        def _(s):
            for f in P.ops["ac"]:
                f(s)

    return nc


def _in_maps(inputs):
    consts = _make_consts()
    x = np.asarray(inputs["L0_attn_query"]).reshape(T, DM)
    wq = np.asarray(inputs["L0_attn_Wq"])
    wk = np.asarray(inputs["L0_attn_Wk"])
    wv = np.asarray(inputs["L0_attn_Wv"])
    lam = np.concatenate([
        np.asarray(inputs["L0_attn_lambda_q1"]), np.asarray(inputs["L0_attn_lambda_k1"]),
        np.asarray(inputs["L0_attn_lambda_q2"]), np.asarray(inputs["L0_attn_lambda_k2"]),
    ]).reshape(1, 256).astype(np.float32)
    rmsw = np.asarray(inputs["L0_attn_rms_weight"]).reshape(128, 1).astype(np.float32)
    shared = {
        "wout": np.ascontiguousarray(np.asarray(inputs["L0_attn_Wout"], np.float32)),
        "gate": np.ascontiguousarray(np.asarray(inputs["L0_ffn_gate_w"], np.float32)),
        "lamqk": lam,
        "rmsw": rmsw,
        "w1": np.ascontiguousarray(np.asarray(inputs["L0_ffn_w1"], np.float32)),
        "w3": np.ascontiguousarray(np.asarray(inputs["L0_ffn_w3"], np.float32)),
        "w2": np.ascontiguousarray(np.asarray(inputs["L0_ffn_w2"], np.float32)),
        "sw1": np.ascontiguousarray(np.asarray(inputs["L0_ffn_sw1"], np.float32)),
        "sw3": np.ascontiguousarray(np.asarray(inputs["L0_ffn_sw3"], np.float32)),
        "sw2": np.ascontiguousarray(np.asarray(inputs["L0_ffn_sw2"], np.float32)),
    }
    shared.update(consts)
    maps = []
    pr = np.arange(128, dtype=np.int32)
    for c in range(NCORE):
        m = dict(shared)
        m["x"] = np.ascontiguousarray(x[TL * c:TL * (c + 1), :], dtype=np.float32)
        m["wq"] = np.ascontiguousarray(wq[:, CH * c:CH * (c + 1)], dtype=np.float32)
        m["wk"] = np.ascontiguousarray(wk[:, CH * c:CH * (c + 1)], dtype=np.float32)
        m["wv"] = np.ascontiguousarray(wv[:, CH * c:CH * (c + 1)], dtype=np.float32)
        gidx = np.zeros((128, NT), np.int32)
        for k in range(NT):
            r, i = k // NTL, k % NTL
            gidx[:, k] = T * r + TL * c + 128 * i + pr
        m["gidx"] = gidx
        xidx = np.zeros((128, NDC), np.int32)
        for jj in range(NDC):
            xidx[:, jj] = DM * c + 128 * jj + pr
        m["xidx"] = xidx
        maps.append(m)
    return maps


def kernel(**inputs):
    if "nc" not in _NC_CACHE:
        _NC_CACHE["nc"] = _build(_DEBUG_TAPS)
    nc = _NC_CACHE["nc"]
    maps = _in_maps(inputs)
    res = run_bass_kernel_spmd(nc, maps, core_ids=list(range(NCORE)))
    _NC_CACHE["last_res"] = res
    y = np.concatenate([res.results[c]["y"] for c in range(NCORE)], axis=0)
    return y.reshape(1, T, DM).astype(np.float32)


# version 19 (final)
# speedup vs baseline: 1.0279x; 1.0279x over previous
import math
from contextlib import ExitStack

import numpy as np
import ml_dtypes

import concourse.bass as bass
import concourse.mybir as mybir
from concourse.bass_core_rust import IndirectOffsetOnAxis
from concourse.bass_utils import run_bass_kernel_spmd

F32 = mybir.dt.float32
BF16 = mybir.dt.bfloat16
I32 = mybir.dt.int32
AF = mybir.ActivationFunctionType
ALU = mybir.AluOpType
AX = mybir.AxisListType
RG = [[0, 1, 2, 3, 4, 5, 6, 7]]

NCORE = 8
T = 8192
TL = T // NCORE
DM = 2048
CH = 256
NI = 1408
NSH = 2816
NE = 8
LAMBDA_INIT = 0.8 - 0.6 * math.exp(-0.3 * 12)
EPS = 1e-5

NT = T // 128
NTL = TL // 128
NDC = DM // 128
NIC = NI // 128
_BF = ml_dtypes.bfloat16

O_OFF = [0, 129, 258, 512, 641, 770, 1024, 1153]

_DEBUG_TAPS = False
_NC_CACHE = {}


def _make_consts():
    c = {}
    c["c_id32"] = np.eye(128, dtype=np.float32)
    c["c_idbf"] = np.eye(128).astype(_BF)
    nf = T // 2 + 1
    g = np.fft.irfft(np.exp(-0.5 * (np.linspace(0.0, 1.0, nf) / 0.15) ** 2), n=T)
    GB = np.zeros((128, 384))
    s = np.arange(128)[:, None]
    o = np.arange(128)[None, :]
    for off, gi in ((128, 0), (0, 1), (-128, 2)):
        d = o - s + off
        GB[:, 128 * gi:128 * (gi + 1)] = np.where(np.abs(d) <= 64, g[d % T], 0.0)
    c["c_gband"] = GB.astype(_BF)
    inv_freq = 1.0 / 10000.0 ** (np.arange(0, 64, 2, dtype=np.float64) / 64)
    fr = np.outer(np.arange(T, dtype=np.float64), inv_freq)
    c["c_cosr"] = np.cos(fr).reshape(NT, 128, 32).transpose(1, 0, 2).reshape(128, NT * 32).astype(_BF)
    c["c_sinr"] = np.sin(fr).reshape(NT, 128, 32).transpose(1, 0, 2).reshape(128, NT * 32).astype(_BF)
    kl = np.arange(128)[:, None]
    ql = np.arange(512)[None, :]
    m = np.zeros((128, 2048), dtype=np.float32)
    for r in range(4):
        m[:, 512 * r:512 * (r + 1)] = np.where(ql < kl + 128 * r, -1e30, 0.0)
    c["c_mask"] = m.astype(_BF)
    c["c_psel"] = np.ones((1, 128), dtype=np.float32)
    c["c_onebf"] = np.ones((1, 128), dtype=np.float32).astype(_BF)
    sc = np.zeros((128, 8), dtype=np.float32)
    sc[:, 0] = EPS
    sc[:, 1] = -0.5
    sc[:, 2] = 0.5
    sc[:, 3] = 1.0 / 128.0
    sc[:, 4] = math.log(0.125)
    c["c_scal"] = sc
    return c


class _Sem:
    def __init__(self, h):
        self.h = h
        self.n = 0

    def inc(self, k=1):
        self.n += k
        return self.n


class _Plan:
    ENGS = ("sy", "po", "pe", "dv", "ac")

    def __init__(self):
        self.ops = {e: [] for e in self.ENGS}
        self.marks = {}
        self.selfsems = {}

    def mark(self, name):
        self.marks[name] = {e: len(self.ops[e]) for e in self.ENGS}

    def truncate(self, name):
        if name is None or name not in self.marks:
            return
        m = self.marks[name]
        for e in self.ENGS:
            del self.ops[e][m[e]:]

    def wait(self, eng, sem, val):
        if val is None or val <= 0:
            return
        self.ops[eng].append(lambda e, s=sem.h, v=val: e.wait_ge(s, v))

    def call(self, eng, name, *args, inc=None, incv=1, dep=True, **kw):
        selfsem = self.selfsems.get(eng)
        if selfsem is not None and not dep and inc is not None:
            tok = inc.inc(incv)

            def fn(e, name=name, args=args, kw=kw, li=inc.h, lv=incv):
                getattr(e, name)(*args, **kw).then_inc(li, lv)
            self.ops[eng].append(fn)
            return tok
        if selfsem is not None:
            n_prev = selfsem.n
            if dep and n_prev > 0:
                self.ops[eng].append(
                    lambda e, s=selfsem.h, v=n_prev: e.wait_ge(s, v))
            n_now = selfsem.inc()

            def fn(e, name=name, args=args, kw=kw, s=selfsem.h):
                getattr(e, name)(*args, **kw).then_inc(s, 1)
            self.ops[eng].append(fn)
            tok = None
            if inc is not None:
                tok = inc.inc(incv)
                self.ops[eng].append(
                    lambda e, s=selfsem.h, v=n_now, li=inc.h, lv=incv:
                    (e.wait_ge(s, v), e.sem_inc(li, lv)))
            return tok

        tok = inc.inc(incv) if inc is not None else None

        def fn(e, name=name, args=args, kw=kw, inc=inc, incv=incv):
            inst = getattr(e, name)(*args, **kw)
            if inc is not None:
                inst.then_inc(inc.h, incv)
        self.ops[eng].append(fn)
        return tok

    def dma(self, eng, out, in_, sem, **kw):
        return self.call(eng, "dma_start", out=out, in_=in_, inc=sem, incv=16, **kw)


def _build(debug_taps=False, stop=None):
    nc = bass.Bass("TRN2", target_bir_lowering=False)

    def din(name, shape, dt=F32):
        return nc.dram_tensor(name, list(shape), dt, kind="ExternalInput").ap()

    x_in = din("x", (TL, DM))
    wq_in = din("wq", (DM, CH))
    wk_in = din("wk", (DM, CH))
    wv_in = din("wv", (DM, CH))
    wout_in = din("wout", (DM, DM))
    gate_in = din("gate", (NE, DM))
    lam_in = din("lamqk", (1, 256))
    rw_in = din("rmsw", (128, 1))
    w1_in = din("w1", (NE, DM, NI))
    w3_in = din("w3", (NE, DM, NI))
    w2_in = din("w2", (NE, NI, DM))
    sw1_in = din("sw1", (DM, NSH))
    sw3_in = din("sw3", (DM, NSH))
    sw2_in = din("sw2", (NSH, DM))
    gidx_in = din("gidx", (128, NT), I32)
    xidx_in = din("xidx", (128, NDC), I32)
    cid32_in = din("c_id32", (128, 128))
    cidbf_in = din("c_idbf", (128, 128), BF16)
    cg_in = din("c_gband", (128, 384), BF16)
    ccos_in = din("c_cosr", (128, NT * 32), BF16)
    csin_in = din("c_sinr", (128, NT * 32), BF16)
    cmask_in = din("c_mask", (128, 2048), BF16)
    cpsel_in = din("c_psel", (1, 128))
    conebf_in = din("c_onebf", (1, 128), BF16)
    cscal_in = din("c_scal", (128, 8))

    y_out = nc.dram_tensor("y", [TL, DM], F32, kind="ExternalOutput").ap()

    xnF_sh = nc.dram_tensor("xnF_sh", [DM, TL], BF16, kind="Internal").ap()
    xnF_all = nc.dram_tensor("xnF_all", [NCORE * DM, TL], BF16, kind="Internal").ap()
    aR_sh = nc.dram_tensor("aR_sh", [T, CH], BF16, kind="Internal").ap()
    aR_all = nc.dram_tensor("aR_all", [NCORE * T, CH], BF16, kind="Internal").ap()

    taps = {}
    if debug_taps:
        def tout(name, shape, dt):
            taps[name] = nc.dram_tensor(name, list(shape), dt, kind="ExternalOutput").ap()
        tout("t_xnF", (128, NDC * TL), BF16)
        tout("t_coeff", (1, NE * TL), BF16)
        tout("t_lam", (128, 1), F32)
        tout("t_QF", (128, 2 * T), BF16)
        tout("t_KF", (128, 2 * T), BF16)
        tout("t_V", (128, NT * 258), BF16)
        tout("t_negc", (128, 4), F32)
        tout("t_aD", (128, NT * CH), BF16)
        tout("t_outF_b", (128, NDC * TL), F32)
        tout("t_outF_d", (128, NDC * TL), F32)

    P = _Plan()
    ctx = ExitStack()
    with ctx:
        E = ctx.enter_context

        S = {}
        for sname in ["s_dvS", "s_acS",
                      "s_c", "s_xa0", "s_xa1", "s_xs0", "s_xs1", "s_wA", "s_wB",
                      "s_ws0", "s_ws1", "s_ws2", "s_ws3", "s_wg0", "s_wg1",
                      "s_wx", "s_st0", "s_st1", "s_stx", "s_cc",
                      "s_pe", "s_dv", "s_ac", "s_pe2", "s_dv2", "s_ac2"]:
            S[sname] = _Sem(E(nc.semaphore(sname)))
        s_c, s_cc = S["s_c"], S["s_cc"]
        s_xa = [S["s_xa0"], S["s_xa1"]]
        s_xs = [S["s_xs0"], S["s_xs1"]]
        s_wA, s_wB, s_wx, s_stx = S["s_wA"], S["s_wB"], S["s_wx"], S["s_stx"]
        s_ws = [S["s_ws0"], S["s_ws1"], S["s_ws2"], S["s_ws3"]]
        s_wg = [S["s_wg0"], S["s_wg1"]]
        s_st = [S["s_st0"], S["s_st1"]]
        s_pe, s_dv, s_ac = S["s_pe"], S["s_dv"], S["s_ac"]
        s_pe2, s_dv2, s_ac2 = S["s_pe2"], S["s_dv2"], S["s_ac2"]
        P.selfsems = {"dv": S["s_dvS"], "ac": S["s_acS"]}

        def barrier():
            toks = [("pe", s_pe, s_pe.n), ("pe", s_pe2, s_pe2.n),
                    ("dv", s_dv, s_dv.n), ("dv", s_dv2, s_dv2.n),
                    ("ac", s_ac, s_ac.n), ("ac", s_ac2, s_ac2.n)]
            for eng in ("pe", "dv", "ac", "sy", "po"):
                for peng, sem, tok in toks:
                    if peng != eng:
                        P.wait(eng, sem, tok)

        id32 = E(nc.sbuf_tensor("id32", [128, 128], F32))
        idbf = E(nc.sbuf_tensor("idbf", [128, 128], BF16))
        gband = E(nc.sbuf_tensor("gband", [128, 384], BF16))
        psel = E(nc.sbuf_tensor("psel", [1, 128], F32))
        onebf = E(nc.sbuf_tensor("onebf", [1, 128], BF16))
        outF = E(nc.sbuf_tensor("outF", [128, NDC * TL], F32))
        lam_col = E(nc.sbuf_tensor("lam_col", [128, 1], F32))
        rw_col = E(nc.sbuf_tensor("rw_col", [128, 1], F32))
        negc_cols = E(nc.sbuf_tensor("negc_cols", [128, 4], F32))
        smallA = E(nc.sbuf_tensor("smallA", [128, 32], F32))
        small1 = E(nc.sbuf_tensor("small1", [1, 136], F32))
        sgidx = E(nc.sbuf_tensor("sgidx", [128, NT], I32))
        sxidx = E(nc.sbuf_tensor("sxidx", [128, NDC], I32))
        cscal = E(nc.sbuf_tensor("cscal", [128, 8], F32))
        tab_scope = ExitStack()
        cosr = tab_scope.enter_context(nc.sbuf_tensor("cosr", [128, NT * 32], BF16, side="right"))
        sinr = tab_scope.enter_context(nc.sbuf_tensor("sinr", [128, NT * 32], BF16, side="right"))
        maskt = E(nc.sbuf_tensor("maskt", [128, 2048], BF16))

        tok_c = 0
        for ap_in, tile in [(cid32_in, id32), (cidbf_in, idbf), (cg_in, gband),
                            (cpsel_in, psel),
                            (conebf_in, onebf), (cscal_in, cscal),
                            (ccos_in, cosr), (csin_in, sinr), (cmask_in, maskt)]:
            tok_c = P.dma("sy", tile[:, :], ap_in, s_c)

        ab_scope = ExitStack()
        Eab = ab_scope.enter_context
        xnF_loc = Eab(nc.sbuf_tensor("xnF_loc", [128, NDC * TL], BF16))
        coeffT = Eab(nc.sbuf_tensor("coeffT", [1, NE * TL], BF16))

        with ExitStack() as pa:
            Ea = pa.enter_context
            xt = [Ea(nc.sbuf_tensor(f"xt{i}", [128, DM], F32)) for i in range(2)]
            xsq = Ea(nc.sbuf_tensor("xsq", [128, DM], F32))
            xnb = Ea(nc.sbuf_tensor("xnb", [128, DM], BF16))
            stats = Ea(nc.sbuf_tensor("stats", [128, 8], F32))
            sgw = Ea(nc.sbuf_tensor("sgw", [NE, DM], BF16))
            gwT = Ea(nc.sbuf_tensor("gwT", [128, NDC * NE], BF16))
            lgF = Ea(nc.sbuf_tensor("lgF", [NE, TL], F32))
            lgtok = Ea(nc.sbuf_tensor("lgtok", [128, NE], F32))
            gw8 = Ea(nc.sbuf_tensor("gw8", [128, NE], F32))
            slam = Ea(nc.sbuf_tensor("slam", [1, 256], F32))
            ptp = Ea(nc.psum_tensor("ptp", [128, 512], F32))
            ptpb = Ea(nc.psum_tensor("ptpb", [128, 512], BF16))
            pgl = Ea(nc.psum_tensor("pgl", [NE, TL], F32))

            P.dma("po", sgidx[:, :], gidx_in, s_wA)
            P.dma("po", sxidx[:, :], xidx_in, s_wA)
            P.dma("po", sgw[:, :], gate_in, s_wA)
            P.dma("po", slam[:, :], lam_in, s_wA)
            t_wA = P.dma("po", rw_col[:, :], rw_in, s_wA)

            P.wait("dv", s_wA, t_wA)
            P.call("dv", "tensor_mul", small1[0:1, 0:64], slam[0:1, 0:64], slam[0:1, 64:128])
            P.call("dv", "tensor_mul", small1[0:1, 64:128], slam[0:1, 128:192],
                   slam[0:1, 192:256])
            P.call("dv", "reduce_sum", small1[0:1, 128:129], small1[0:1, 0:64], axis=AX.X)
            t0 = P.call("dv", "reduce_sum", small1[0:1, 129:130], small1[0:1, 64:128],
                        axis=AX.X, inc=s_dv)
            P.wait("ac", s_c, tok_c)
            P.wait("ac", s_dv, t0)
            t1 = P.call("ac", "activation", small1[0:1, 130:132], small1[0:1, 128:130],
                        AF.Exp, inc=s_ac)
            P.wait("dv", s_ac, t1)
            P.call("dv", "tensor_sub", small1[0:1, 132:133], small1[0:1, 130:131],
                   small1[0:1, 131:132])
            t2 = P.call("dv", "tensor_scalar_add", small1[0:1, 132:133],
                        small1[0:1, 132:133], float(LAMBDA_INIT), inc=s_dv)
            P.wait("pe", s_c, tok_c)
            P.wait("pe", s_dv, t2)
            t3 = P.call("pe", "matmul", ptp[:, 0:1], psel[0:1, 0:128],
                        small1[0:1, 132:133], start=True, stop=True, inc=s_pe)
            P.wait("dv", s_pe, t3)
            P.call("dv", "tensor_copy", lam_col[:, :], ptp[:, 0:1])
            P.wait("dv", s_wA, t_wA)
            P.call("dv", "tensor_scalar_mul", rw_col[:, :], rw_col[:, :],
                   float(1.0 - LAMBDA_INIT))

            xtok = [0] * NTL
            dv_done = [0] * NTL
            pe_xnb = [0] * NTL
            for i in range(NTL):
                if i >= 2:
                    P.wait("sy", s_dv, dv_done[i - 2])
                xtok[i] = P.dma("sy", xt[i % 2][:, :], x_in[128 * i:128 * (i + 1), :],
                                s_xa[i % 2])
                P.wait("dv", s_xa[i % 2], xtok[i])
                if i >= 1:
                    P.wait("dv", s_pe, pe_xnb[i - 1])
                P.call("dv", "reduce_sum", stats[:, 0:1], xt[i % 2][:, :], axis=AX.X,
                       negate=True)
                P.call("dv", "tensor_mul", xsq[:, :], xt[i % 2][:, :], xt[i % 2][:, :])
                P.call("dv", "reduce_sum", stats[:, 1:2], xsq[:, :], axis=AX.X)
                P.call("dv", "tensor_scalar_mul", stats[:, 2:3], stats[:, 0:1], 1.0 / DM)
                P.call("dv", "tensor_mul", stats[:, 3:4], stats[:, 2:3], stats[:, 2:3])
                P.call("dv", "tensor_scalar_mul", stats[:, 4:5], stats[:, 1:2], 1.0 / DM)
                tv = P.call("dv", "tensor_sub", stats[:, 5:6], stats[:, 4:5],
                            stats[:, 3:4], inc=s_dv)
                P.wait("ac", s_dv, tv)
                P.call("ac", "activation", stats[:, 6:7], stats[:, 5:6], AF.Ln,
                       bias=cscal[:, 0:1])
                tr = P.call("ac", "activation", stats[:, 7:8], stats[:, 6:7], AF.Exp,
                            scale=cscal[:, 1:2], inc=s_ac)
                P.wait("dv", s_ac, tr)
                dv_done[i] = P.call("dv", "tensor_scalar", xnb[:, :], xt[i % 2][:, :],
                                    stats[:, 2:3], stats[:, 7:8],
                                    op0=ALU.add, op1=ALU.mult, inc=s_dv)
                P.wait("pe", s_dv, dv_done[i])
                for grp in range(4):
                    if i > 0 or grp > 0:
                        P.wait("pe", s_dv2, s_dv2.n)
                    for j in range(4):
                        jj = 4 * grp + j
                        pe_xnb[i] = P.call("pe", "transpose",
                                           ptpb[:, 128 * j:128 * (j + 1)],
                                           xnb[:, 128 * jj:128 * (jj + 1)], idbf[:, :],
                                           inc=(s_pe if j == 3 else None)) or pe_xnb[i]
                    P.wait("dv", s_pe, s_pe.n)
                    for j in range(4):
                        jj = 4 * grp + j
                        P.call("dv", "tensor_copy",
                               xnF_loc[:, TL * jj + 128 * i: TL * jj + 128 * (i + 1)],
                               ptpb[:, 128 * j:128 * (j + 1)],
                               inc=(s_dv2 if j == 3 else None))
            t_xnF = s_dv2.n

            P.wait("sy", s_dv2, t_xnF)
            t_sh = P.dma("sy", xnF_sh.rearrange("(c p) t -> p c t", p=128),
                         xnF_loc[:, :].rearrange("p (c t) -> p c t", t=TL), s_stx)
            P.wait("po", s_stx, t_sh)
            t_ag1 = P.call("po", "collective_compute", "AllGather", ALU.bypass, RG,
                           [xnF_sh.opt()], [xnF_all.opt()], inc=s_cc)

            P.wait("pe", s_wA, t_wA)
            for grp in range(4):
                P.wait("pe", s_dv2, s_dv2.n)
                for j in range(4):
                    jj = 4 * grp + j
                    P.call("pe", "transpose", ptpb[:, 128 * j:128 * j + NE],
                           sgw[:, 128 * jj:128 * (jj + 1)], idbf[0:NE, 0:NE],
                           inc=(s_pe if j == 3 else None))
                P.wait("dv", s_pe, s_pe.n)
                for j in range(4):
                    jj = 4 * grp + j
                    P.call("dv", "tensor_copy", gwT[:, NE * jj:NE * (jj + 1)],
                           ptpb[:, 128 * j:128 * j + NE],
                           inc=(s_dv2 if j == 3 else None))
            t_gwT = s_dv2.n
            P.wait("pe", s_dv2, t_gwT)
            for tb in range(2):
                for jj in range(NDC):
                    P.call("pe", "matmul", pgl[:, 512 * tb:512 * (tb + 1)],
                           gwT[:, NE * jj:NE * (jj + 1)],
                           xnF_loc[:, TL * jj + 512 * tb: TL * jj + 512 * (tb + 1)],
                           start=(jj == 0), stop=(jj == NDC - 1),
                           inc=(s_pe if (tb == 1 and jj == NDC - 1) else None))
            t_gl = s_pe.n
            P.wait("dv", s_pe, t_gl)
            t_lgF = P.call("dv", "tensor_copy", lgF[:, :], pgl[:, :], inc=s_dv)
            P.wait("pe", s_dv, t_lgF)
            for i in range(NTL):
                tk = P.call("pe", "transpose", ptp[:, 0:NE],
                            lgF[:, 128 * i:128 * (i + 1)], id32[0:NE, 0:NE], inc=s_pe)
                P.wait("dv", s_pe, tk)
                P.call("dv", "tensor_copy", lgtok[:, :], ptp[:, 0:NE])
                tm = P.call("dv", "reduce_max", smallA[:, 0:1], lgtok[:, :], axis=AX.X,
                            negate=True, inc=s_dv)
                P.wait("ac", s_dv, tm)
                te = P.call("ac", "activation", gw8[:, :], lgtok[:, :], AF.Exp,
                            bias=smallA[:, 0:1], inc=s_ac)
                P.wait("dv", s_ac, te)
                P.call("dv", "reduce_max", smallA[:, 1:2], gw8[:, :], axis=AX.X)
                P.call("dv", "tensor_scalar", smallA[:, 8:16], gw8[:, :],
                       smallA[:, 1:2], None, op0=ALU.is_ge)
                P.call("dv", "tensor_mul", smallA[:, 16:24], gw8[:, :], smallA[:, 8:16])
                P.call("dv", "tensor_sub", smallA[:, 16:24], gw8[:, :], smallA[:, 16:24])
                P.call("dv", "reduce_max", smallA[:, 2:3], smallA[:, 16:24], axis=AX.X)
                P.call("dv", "tensor_scalar", smallA[:, 8:16], gw8[:, :],
                       smallA[:, 2:3], None, op0=ALU.is_ge)
                P.call("dv", "tensor_add", smallA[:, 3:4], smallA[:, 1:2], smallA[:, 2:3])
                P.call("dv", "reciprocal", smallA[:, 4:5], smallA[:, 3:4])
                P.call("dv", "tensor_mul", smallA[:, 8:16], gw8[:, :], smallA[:, 8:16])
                tc = P.call("dv", "tensor_scalar_mul", smallA[:, 8:16], smallA[:, 8:16],
                            smallA[:, 4:5], inc=s_dv)
                P.wait("pe", s_dv, tc)
                for e in range(NE):
                    P.wait("pe", s_dv2, s_dv2.n)
                    tk = P.call("pe", "transpose", ptp[0:1, 128:256],
                                smallA[:, 8 + e:9 + e], id32[:, :], inc=s_pe)
                    P.wait("dv", s_pe, tk)
                    P.call("dv", "tensor_copy",
                           coeffT[0:1, TL * e + 128 * i: TL * e + 128 * (i + 1)],
                           ptp[0:1, 128:256], inc=s_dv2)
            t_coeff = s_dv2.n

            if debug_taps:
                P.wait("sy", s_dv2, t_coeff)
                P.dma("sy", taps["t_xnF"], xnF_loc[:, :], s_stx)
                P.dma("sy", taps["t_coeff"], coeffT[:, :], s_stx)
                P.dma("sy", taps["t_lam"], lam_col[:, :], s_stx)

        P.mark("A")
        barrier()
        P.mark("Abar")

        with ExitStack() as pb:
            Eb = pb.enter_context
            wring = Eb(nc.sbuf_tensor("wring", [128, 4 * 6144], BF16))
            m_e = Eb(nc.sbuf_tensor("m_e", [128, NIC * TL], BF16))
            ssl = Eb(nc.sbuf_tensor("ssl", [128, TL], F32))
            tg = Eb(nc.sbuf_tensor("tg", [128, TL], BF16))
            cb = Eb(nc.sbuf_tensor("cb", [128, TL], BF16))
            ph = Eb(nc.psum_tensor("ph", [128, TL], F32))
            pg = Eb(nc.psum_tensor("pg", [128, TL], F32))
            py = Eb(nc.psum_tensor("py", [128, TL], F32))
            pcb = Eb(nc.psum_tensor("pcb", [128, TL], F32))

            QW = [(0, 384), (384, 768), (768, 1152), (1152, 1408)]

            slot_free = [0, 0, 0, 0]
            t_cb = 0
            t_prev_y = 0
            for e in range(NE):
                ld = {}

                def eload(tag, slot, src, ncols):
                    P.wait("po", s_pe2, slot_free[slot])
                    dst = wring[:, 6144 * slot:6144 * slot + ncols]
                    if tag[0] in ("w1", "w3"):
                        width = ncols // NDC
                        dst = dst.rearrange("p (c i) -> p c i", i=width)
                    else:
                        dst = dst.rearrange("p (c d) -> p c d", d=DM)
                    ld[tag] = (s_ws[slot], P.dma("po", dst, src, s_ws[slot]))

                def w13src(mat, qi):
                    i0, i1 = QW[qi]
                    t = w1_in if mat == "w1" else w3_in
                    return t[e, :, i0:i1].rearrange("(c p) i -> p c i", p=128)

                for qi in (0, 1):
                    eload(("w1", qi), 2 * qi, w13src("w1", qi), NDC * (QW[qi][1] - QW[qi][0]))
                    eload(("w3", qi), 2 * qi + 1, w13src("w3", qi), NDC * (QW[qi][1] - QW[qi][0]))

                if e == 0:
                    P.wait("pe", s_c, tok_c)
                P.wait("pe", s_dv, t_cb)
                for tb in range(2):
                    P.call("pe", "matmul", pcb[:, 512 * tb:512 * (tb + 1)],
                           onebf[0:1, :],
                           coeffT[0:1, TL * e + 512 * tb: TL * e + 512 * (tb + 1)],
                           start=True, stop=True,
                           inc=(s_pe if tb == 1 else None))
                P.wait("dv", s_pe, s_pe.n)
                t_cb = P.call("dv", "tensor_copy", cb[:, :], pcb[:, :], inc=s_dv)
                if e == 0:
                    P.mark("Bcb")

                for ic in range(NIC):
                    qi = min(ic // 3, 3)
                    icl = ic - 3 * qi
                    s1, s3 = (2 * qi) % 4, (2 * qi + 1) % 4
                    qw = QW[qi][1] - QW[qi][0]
                    P.wait("pe", *ld[("w1", qi)])
                    P.wait("pe", *ld[("w3", qi)])
                    if ic >= 1 or e >= 1:
                        P.wait("pe", s_ac, s_ac.n)
                        P.wait("pe", s_dv, s_dv.n)
                        P.wait("pe", s_dv2, s_dv2.n)
                    t_h = 0
                    for (sslot, pdst, lastfl) in ((s1, ph, False), (s3, pg, True)):
                        for tb in range(2):
                            for jj in range(NDC):
                                P.call("pe", "matmul", pdst[:, 512 * tb:512 * (tb + 1)],
                                       wring[:, 6144 * sslot + qw * jj + 128 * icl:
                                             6144 * sslot + qw * jj + 128 * (icl + 1)],
                                       xnF_loc[:, TL * jj + 512 * tb: TL * jj + 512 * (tb + 1)],
                                       start=(jj == 0), stop=(jj == NDC - 1),
                                       inc=(s_pe2 if (tb == 1 and jj == NDC - 1) else None))
                        if not lastfl:
                            t_h = s_pe2.n
                    t_hg = s_pe2.n
                    slot_free[s1] = max(slot_free[s1], t_hg)
                    slot_free[s3] = max(slot_free[s3], t_hg)
                    if ic == 2:
                        eload(("w1", 2), 0, w13src("w1", 2), NDC * 384)
                        eload(("w3", 2), 1, w13src("w3", 2), NDC * 384)
                    if ic == 5:
                        eload(("w1", 3), 2, w13src("w1", 3), NDC * 256)
                        eload(("w3", 3), 3, w13src("w3", 3), NDC * 256)
                    if ic == 8:
                        P.wait("po", s_pe2, max(slot_free[0], slot_free[1]))
                        ld["w2a"] = (s_ws[0], P.dma(
                            "po",
                            wring[:, 0:6 * DM].rearrange("p (c d) -> p c d", d=DM),
                            w2_in[e, 0:768, :].rearrange("(c p) d -> p c d", p=128),
                            s_ws[0]))
                    if ic == 10:
                        P.wait("po", s_pe2, max(slot_free[2], slot_free[3]))
                        ld["w2b"] = (s_ws[2], P.dma(
                            "po",
                            wring[:, 2 * 6144:2 * 6144 + 5 * DM].rearrange(
                                "p (c d) -> p c d", d=DM),
                            w2_in[e, 768:1408, :].rearrange("(c p) d -> p c d", p=128),
                            s_ws[2]))
                    P.wait("ac", s_pe2, t_h)
                    t_sl = P.call("ac", "activation", ssl[:, :], ph[:, :], AF.Silu,
                                  inc=s_ac)
                    P.wait("dv", s_pe2, t_hg)
                    P.wait("dv", s_ac, t_sl)
                    P.wait("dv", s_dv, t_cb)
                    if e > 0 and ic == 0:
                        P.wait("dv", s_pe2, t_prev_y)
                    P.call("dv", "tensor_mul", tg[:, :], pg[:, :], cb[:, :], inc=s_dv)
                    P.call("dv", "tensor_mul", m_e[:, TL * ic:TL * (ic + 1)],
                           tg[:, :], ssl[:, :], inc=s_dv2)
                    if e == 0 and ic == 0:
                        P.mark("Bh0")
                t_m = s_dv2.n
                if e == 0:
                    P.mark("Bm0")

                P.wait("pe", s_dv2, t_m)
                ytoks = []
                t_half0 = 0
                for half in range(2):
                    P.wait("pe", *ld["w2a" if half == 0 else "w2b"])
                    icr = range(0, 6) if half == 0 else range(6, NIC)
                    base = 0 if half == 0 else 2 * 6144
                    for jd in range(NDC):
                        for tb in range(2):
                            slot = (2 * jd + tb) % 2
                            if len(ytoks) >= 2:
                                P.wait("pe", s_dv, ytoks[-2])
                            for ii, ic in enumerate(icr):
                                lhs = wring[:, base + DM * (ic - (0 if half == 0 else 6)) + 128 * jd:
                                            base + DM * (ic - (0 if half == 0 else 6)) + 128 * (jd + 1)]
                                P.call("pe", "matmul", py[:, 512 * slot:512 * (slot + 1)],
                                       lhs,
                                       m_e[:, TL * ic + 512 * tb: TL * ic + 512 * (tb + 1)],
                                       start=(ii == 0), stop=(ii == len(icr) - 1),
                                       inc=(s_pe2 if ii == len(icr) - 1 else None))
                            t_y = s_pe2.n
                            P.wait("dv", s_pe2, t_y)
                            dst = outF[:, TL * jd + 512 * tb: TL * jd + 512 * (tb + 1)]
                            if e == 0 and half == 0:
                                P.call("dv", "tensor_copy", dst,
                                       py[:, 512 * slot:512 * (slot + 1)], inc=s_dv)
                            else:
                                P.call("dv", "tensor_add", dst, dst,
                                       py[:, 512 * slot:512 * (slot + 1)], inc=s_dv)
                            ytoks.append(s_dv.n)
                    if half == 0:
                        t_half0 = s_pe2.n
                t_prev_y = s_pe2.n
                slot_free[0] = max(slot_free[0], t_half0)
                slot_free[1] = max(slot_free[1], t_half0)
                slot_free[2] = max(slot_free[2], t_prev_y)
                slot_free[3] = max(slot_free[3], t_prev_y)
                if e == 0:
                    P.mark("By0")
                if e == 1:
                    P.mark("By1")

            if debug_taps:
                P.wait("sy", s_dv, s_dv.n)
                P.dma("sy", taps["t_outF_b"], outF[:, :], s_stx)

        P.mark("B")
        ab_scope.close()
        barrier()

        with ExitStack() as pcs:
            Ec = pcs.enter_context
            QF = Ec(nc.sbuf_tensor("QF", [128, 2 * T], BF16))
            KF = Ec(nc.sbuf_tensor("KF", [128, 2 * T], BF16))
            Vt = Ec(nc.sbuf_tensor("Vt", [128, NT * 258], BF16))

            with ExitStack() as pc1:
                E1 = pc1.enter_context
                xs = E1(nc.sbuf_tensor("xs", [128, DM], BF16))
                wqk = E1(nc.sbuf_tensor("wqk", [128, NDC * 2 * CH], BF16))
                wvs = E1(nc.sbuf_tensor("wvs", [128, NDC * CH], BF16))
                qr = E1(nc.sbuf_tensor("qr", [128, 512], BF16))
                kr = E1(nc.sbuf_tensor("kr", [128, 512], BF16))
                rsc = E1(nc.sbuf_tensor("rsc", [128, 256], BF16))
                pq = E1(nc.psum_tensor("pq", [128, 1024], F32))
                pv = E1(nc.psum_tensor("pv", [128, 1024], F32))
                ptb = E1(nc.psum_tensor("ptb", [128, 512], BF16))

                P.dma("po", wqk[:, :].rearrange("p (c x h) -> p c x h", x=2, h=CH)[:, :, 0, :],
                      wq_in.rearrange("(c p) h -> p c h", p=128), s_wB)
                P.dma("po", wqk[:, :].rearrange("p (c x h) -> p c x h", x=2, h=CH)[:, :, 1, :],
                      wk_in.rearrange("(c p) h -> p c h", p=128), s_wB)
                twqkv = P.dma("po", wvs[:, :].rearrange("p (c h) -> p c h", h=CH),
                              wv_in.rearrange("(c p) h -> p c h", p=128), s_wB)
                P.wait("dv", s_wB, twqkv)
                P.wait("dv", s_c, tok_c)
                t_wqsc = P.call("dv", "tensor_scalar_mul",
                                wqk[:, :].rearrange("p (c x h) -> p c x h", x=2, h=CH)[:, :, 0, :],
                                wqk[:, :].rearrange("p (c x h) -> p c x h", x=2, h=CH)[:, :, 0, :],
                                float(64 ** -0.5), inc=s_dv)
                t_vmemset = P.call("dv", "memset", Vt[:, :], 1.0, inc=s_dv)

                xs_tokA = [0] * NT
                xs_tokB = [0] * NT
                mid_tok = [0] * NT
                qkv_tok = [0] * NT
                rope_tok = [0] * NT
                tp_tok = [0] * NT
                act_tok = [0] * (NT + 1)
                P.wait("sy", s_cc, t_ag1)
                for it in range(NT + 1):
                    i = it
                    j = it - 1
                    if i < NT:
                        r, il = i // NTL, i % NTL
                        if i >= 1:
                            P.wait("sy", s_pe2, mid_tok[i - 1])
                        xs_tokA[i] = P.dma(
                            "sy", xs[:, 0:1024].rearrange("p (c t) -> p c t", t=128),
                            xnF_all[DM * r:DM * r + 1024, 128 * il:128 * (il + 1)]
                            .rearrange("(c p) t -> p c t", p=128), s_xs[0])
                        if i >= 1:
                            P.wait("sy", s_pe2, qkv_tok[i - 1])
                        xs_tokB[i] = P.dma(
                            "sy", xs[:, 1024:2048].rearrange("p (c t) -> p c t", t=128),
                            xnF_all[DM * r + 1024:DM * (r + 1), 128 * il:128 * (il + 1)]
                            .rearrange("(c p) t -> p c t", p=128), s_xs[1])
                        P.wait("pe", s_xs[0], xs_tokA[i])
                        if i == 0:
                            P.wait("pe", s_wB, twqkv)
                            P.wait("pe", s_dv, t_wqsc)
                        if i >= 2:
                            P.wait("pe", s_dv, rope_tok[i - 2])
                            P.wait("pe", s_ac2, act_tok[i - 2])
                        sl = 512 * (i % 2)
                        for jj in range(NDC):
                            if jj == 8:
                                P.wait("pe", s_xs[1], xs_tokB[i])
                            lhs = xs[:, 128 * jj:128 * (jj + 1)]
                            st_, sp_ = (jj == 0), (jj == NDC - 1)
                            P.call("pe", "matmul", pq[:, sl:sl + 512], lhs,
                                   wqk[:, 512 * jj:512 * (jj + 1)], start=st_, stop=sp_)
                            mm_inc = s_pe2 if (jj == 7 or sp_) else None
                            P.call("pe", "matmul", pv[:, sl:sl + 256], lhs,
                                   wvs[:, CH * jj:CH * (jj + 1)], start=st_, stop=sp_,
                                   inc=mm_inc)
                            if jj == 7:
                                mid_tok[i] = s_pe2.n
                        qkv_tok[i] = s_pe2.n

                    if j >= 0:
                        qsl = 256 * (j % 2)
                        P.wait("pe", s_dv, rope_tok[j])
                        if j >= 1:
                            P.wait("pe", s_ac2, act_tok[j])
                        for half, ss in enumerate((qr, qr, kr, kr)):
                            off = qsl + 128 * (half % 2)
                            P.call("pe", "transpose",
                                   ptb[:, 128 * half:128 * (half + 1)],
                                   ss[:, off:off + 128], idbf[:, :],
                                   inc=(s_pe if half == 3 else None))
                        tp_tok[j] = s_pe.n

                    if i < NT:
                        sl = 512 * (i % 2)
                        P.wait("ac", s_pe2, qkv_tok[i])
                        if i == 0:
                            P.wait("ac", s_dv, t_vmemset)
                        vinc = s_ac2 if j < 0 else None
                        P.call("ac", "copy", Vt[:, 258 * i:258 * i + 128],
                               pv[:, sl:sl + 128], dep=False, inc=vinc)
                        P.call("ac", "copy", Vt[:, 258 * i + 129:258 * i + 257],
                               pv[:, sl + 128:sl + 256], dep=False, inc=vinc)
                    if j >= 0:
                        P.wait("ac", s_pe, tp_tok[j])
                        for half in range(4):
                            dstT = (QF if half < 2 else KF)
                            hc = half % 2
                            P.call("ac", "copy",
                                   dstT[:, T * hc + 128 * j: T * hc + 128 * (j + 1)],
                                   ptb[:, 128 * half:128 * (half + 1)], dep=False,
                                   inc=(s_ac2 if half == 3 else None))
                    act_tok[it] = s_ac2.n

                    if i < NT:
                        sl = 512 * (i % 2)
                        qsl = 256 * (i % 2)
                        P.wait("dv", s_pe2, qkv_tok[i])
                        if i >= 2:
                            P.wait("dv", s_pe, tp_tok[i - 2])
                        cosb = cosr[:, 32 * i:32 * (i + 1)].rearrange(
                            "p (o j) -> p o j", o=1).broadcast_to([128, 4, 32])
                        sinb = sinr[:, 32 * i:32 * (i + 1)].rearrange(
                            "p (o j) -> p o j", o=1).broadcast_to([128, 4, 32])
                        for (srcoff, dst) in ((0, qr), (256, kr)):
                            sv = pq[:, sl + srcoff:sl + srcoff + 256].rearrange(
                                "p (s j) -> p s j", s=4)
                            dvw = dst[:, qsl:qsl + 256].rearrange(
                                "p (s j) -> p s j", s=4)
                            ta = rsc[:, 0:128].rearrange("p (s j) -> p s j", s=4)
                            tb_ = rsc[:, 128:256].rearrange("p (s j) -> p s j", s=4)
                            ev = sv[:, :, 0:64:2]
                            od = sv[:, :, 1:64:2]
                            P.call("dv", "tensor_mul", ta, ev, cosb)
                            P.call("dv", "tensor_mul", tb_, od, sinb, dep=False)
                            P.call("dv", "tensor_sub", dvw[:, :, 0:64:2], ta, tb_)
                            P.call("dv", "tensor_mul", ta, ev, sinb)
                            P.call("dv", "tensor_mul", tb_, od, cosb, dep=False)
                            P.call("dv", "tensor_add", dvw[:, :, 1:64:2], ta, tb_,
                                   inc=(s_dv if dst is kr else None))
                        rope_tok[i] = s_dv.n

            with ExitStack() as pcb_s:
                Ecb = pcb_s.enter_context
                s2scr = Ecb(nc.sbuf_tensor("s2scr", [128, 512], BF16))
                nsq = Ecb(nc.sbuf_tensor("nsq", [1, 16], F32))
                pns = Ecb(nc.psum_tensor("pns", [1, 1024], F32))
                pbc = Ecb(nc.psum_tensor("pbc", [128, 4], F32))

                t_qkv_evac = act_tok[NT]
                P.wait("ac", s_ac2, t_qkv_evac)
                P.wait("pe", s_ac2, t_qkv_evac)
                prev_mm = 0
                for mi, MF in ((0, QF), (1, KF)):
                    for slc in range(32):
                        P.wait("ac", s_pe2, prev_mm)
                        tq = P.call("ac", "activation", s2scr[:, :],
                                    MF[:, 512 * slc:512 * (slc + 1)], AF.Square,
                                    inc=s_ac2)
                        P.wait("pe", s_ac2, tq)
                        slot = slc % 2
                        P.wait("pe", s_dv, s_dv.n)
                        prev_mm = P.call("pe", "matmul",
                                         pns[:, 512 * slot:512 * (slot + 1)],
                                         Vt[:, 128:129], s2scr[:, :],
                                         start=True, stop=True, inc=s_pe2)
                        P.wait("dv", s_pe2, prev_mm)
                        col = 2 * mi + slc // 16
                        P.call("dv", "reduce_max", nsq[0:1, 8 + col:9 + col],
                               pns[0:1, 512 * slot:512 * (slot + 1)], axis=AX.X)
                        if slc % 16 == 0:
                            P.call("dv", "tensor_copy", nsq[0:1, col:col + 1],
                                   nsq[0:1, 8 + col:9 + col], inc=s_dv)
                        else:
                            P.call("dv", "tensor_max", nsq[0:1, col:col + 1],
                                   nsq[0:1, col:col + 1], nsq[0:1, 8 + col:9 + col],
                                   inc=s_dv)
                t_ns = P.call("dv", "tensor_mul", nsq[0:1, 4:6], nsq[0:1, 0:2],
                              nsq[0:1, 2:4], inc=s_dv)
                P.wait("ac", s_dv, t_ns)
                P.call("ac", "activation", nsq[0:1, 6:8], nsq[0:1, 4:6], AF.Ln)
                t_c = P.call("ac", "activation", nsq[0:1, 6:8], nsq[0:1, 6:8],
                             AF.Exp, scale=cscal[0:1, 2:3], inc=s_ac)
                P.wait("dv", s_ac, t_c)
                t_negc = P.call("dv", "tensor_scalar_mul", nsq[0:1, 6:8],
                                nsq[0:1, 6:8], -1.0, inc=s_dv)
                P.wait("pe", s_dv, t_negc)
                for h in range(2):
                    P.call("pe", "matmul", pbc[:, h:h + 1],
                           psel[0:1, 0:128], nsq[0:1, 6 + h:7 + h],
                           start=True, stop=True,
                           inc=(s_pe if h == 1 else None))
                t_bc = s_pe.n
                P.wait("dv", s_pe, t_bc)
                t_negc2 = P.call("dv", "tensor_copy", negc_cols[:, 0:2],
                                 pbc[:, 0:2], inc=s_dv)
                if debug_taps:
                    P.wait("sy", s_dv, t_negc2)
                    P.dma("sy", taps["t_QF"], QF[:, :], s_stx)
                    P.dma("sy", taps["t_KF"], KF[:, :], s_stx)
                    P.dma("sy", taps["t_V"], Vt[:, :], s_stx)
                    P.dma("sy", taps["t_negc"], negc_cols[:, :], s_stx)

            P.mark("C1")
            barrier()
            tab_scope.close()

            aD = Ec(nc.sbuf_tensor("aD", [128, NT * CH], BF16))
            with ExitStack() as pc2:
                E2 = pc2.enter_context
                at_all = E2(nc.sbuf_tensor("at_all", [128, 3072], BF16))
                sc1 = E2(nc.sbuf_tensor("sc1", [128, 128], BF16))
                sc2 = E2(nc.sbuf_tensor("sc2", [128, 128], BF16))
                rsum = E2(nc.sbuf_tensor("rsum", [128, 8], F32))
                ps_all = E2(nc.psum_tensor("ps_all", [128, 2048], F32))
                po = E2(nc.psum_tensor("po", [128, 1536], F32))

                t_oevac = 0
                exp_seq = []
                s_seq = []
                av_seq = []

                def emit_S(h, qb, kc, g):
                    slot = kc % 2
                    if g >= 2:
                        P.wait("pe", s_ac, exp_seq[g - 2])
                    diag = kc - 4 * qb
                    on_diag = 0 <= diag <= 3
                    for p in range(2):
                        P.call("pe", "matmul",
                               ps_all[:, 1024 * slot + 512 * p:1024 * slot + 512 * (p + 1)],
                               KF[64 * p:64 * (p + 1),
                                  T * h + 128 * kc:128 * (kc + 1) + T * h],
                               QF[64 * p:64 * (p + 1),
                                  T * h + 512 * qb:512 * (qb + 1) + T * h],
                               start=True, stop=not on_diag, tile_position=(64 * p, 0),
                               inc=(s_pe2 if (p == 1 and not on_diag) else None))
                    if on_diag:
                        for p in range(2):
                            base = 1024 * slot + 512 * p
                            if diag > 0:
                                P.call("pe", "matmul",
                                       ps_all[:, base:base + 128 * diag],
                                       idbf[:, :], maskt[:, 1536:1536 + 128 * diag],
                                       start=False, stop=False,
                                       skip_group_check=True)
                            P.call("pe", "matmul",
                                   ps_all[:, base + 128 * diag:base + 128 * (diag + 1)],
                                   idbf[:, :], maskt[:, 0:128],
                                   start=False, stop=True, skip_group_check=True,
                                   inc=(s_pe2 if p == 1 else None))
                    s_seq.append(s_pe2.n)

                def emit_exp(h, kc, g):
                    slot = kc % 2
                    aslot = g % 3
                    P.wait("ac", s_pe2, s_seq[g])
                    if g >= 3:
                        P.wait("ac", s_pe, av_seq[g - 3])
                    P.call("ac", "activation", at_all[:, 1024 * aslot:1024 * (aslot + 1)],
                           ps_all[:, 1024 * slot:1024 * (slot + 1)], AF.Exp,
                           bias=negc_cols[:, h:h + 1], inc=s_ac, dep=False)
                    exp_seq.append(s_ac.n)

                def emit_AV(h, kc, g, first):
                    aslot = g % 3
                    P.wait("pe", s_ac, exp_seq[g])
                    for p in range(2):
                        for qt in range(4):
                            u = 4 * p + qt
                            P.call("pe", "matmul",
                                   po[:, O_OFF[u]:O_OFF[u] + 129],
                                   at_all[:, 1024 * aslot + 512 * p + 128 * qt:
                                          1024 * aslot + 512 * p + 128 * (qt + 1)],
                                   Vt[:, 258 * kc + 129 * h:258 * kc + 129 * (h + 1)],
                                   start=(first and u in (0, 3, 6)), stop=False,
                                   skip_group_check=True,
                                   inc=(s_pe if (p == 1 and qt == 3) else None))
                    av_seq.append(s_pe.n)

                g0 = 0
                for h in range(2):
                    for qb in range(16):
                        nkc = 4 * qb + 4
                        emit_S(h, qb, 0, g0 + 0)
                        emit_exp(h, 0, g0 + 0)
                        emit_S(h, qb, 1, g0 + 1)
                        emit_exp(h, 1, g0 + 1)
                        for kc in range(nkc):
                            g = g0 + kc
                            if kc + 2 < nkc:
                                emit_S(h, qb, kc + 2, g + 2)
                                emit_exp(h, kc + 2, g + 2)
                            if kc == 0:
                                P.wait("pe", s_dv, t_oevac)
                            emit_AV(h, kc, g, kc == 0)
                        P.wait("dv", s_pe, av_seq[-1])
                        for u in range(8):
                            P.call("dv", "reciprocal", rsum[:, u:u + 1],
                                   po[:, O_OFF[u] + 128:O_OFF[u] + 129], dep=False)
                        for qt in range(4):
                            u1, u2 = qt, 4 + qt
                            P.call("dv", "tensor_scalar_mul", sc1[:, :],
                                   po[:, O_OFF[u1]:O_OFF[u1] + 128],
                                   rsum[:, u1:u1 + 1])
                            P.call("dv", "tensor_scalar", sc2[:, :],
                                   po[:, O_OFF[u2]:O_OFF[u2] + 128],
                                   rsum[:, u2:u2 + 1], lam_col[:, 0:1],
                                   op0=ALU.mult, op1=ALU.mult, dep=False)
                            tile = 4 * qb + qt
                            P.call("dv", "tensor_sub",
                                   aD[:, 256 * tile + 128 * h:256 * tile + 128 * (h + 1)],
                                   sc1[:, :], sc2[:, :],
                                   inc=(s_dv if qt == 3 else None))
                        t_oevac = s_dv.n
                        g0 += nkc
                t_aD = s_dv.n
                P.mark("C2")
                if debug_taps:
                    P.wait("sy", s_dv, t_aD)
                    P.dma("sy", taps["t_aD"], aD[:, :], s_stx)

            with ExitStack() as pc3:
                E3 = pc3.enter_context
                aRst = E3(nc.sbuf_tensor("aRst", [128, 512], BF16))
                sqsc = E3(nc.sbuf_tensor("sqsc", [128, 128], F32))
                ssq = E3(nc.sbuf_tensor("ssq", [128, 4], F32))
                pcv = E3(nc.psum_tensor("pcv", [128, 1024], F32))

                st_tok = [0] * NT
                rms_tok = [0] * NT
                ar_tok = [0] * NT
                P.wait("pe", s_dv, t_aD)
                for k in range(NT):
                    slot = k % 2
                    psl = 512 * slot
                    if k >= 2:
                        P.wait("pe", s_ac, rms_tok[k - 2])
                        P.wait("pe", s_dv, ar_tok[k - 2])
                    for gi2, srck in enumerate(((k - 1) % NT, k, (k + 1) % NT)):
                        P.call("pe", "matmul", pcv[:, psl:psl + 256],
                               gband[:, 128 * gi2:128 * (gi2 + 1)],
                               aD[:, 256 * srck:256 * (srck + 1)],
                               start=(gi2 == 0), stop=(gi2 == 2),
                               inc=(s_pe2 if gi2 == 2 else None))
                    t_cv = s_pe2.n
                    P.wait("ac", s_pe2, t_cv)
                    P.call("ac", "activation", sqsc[:, :],
                           pcv[:, psl:psl + 128], AF.Square,
                           accum_out=ssq[:, 0:1])
                    P.call("ac", "activation", sqsc[:, :],
                           pcv[:, psl + 128:psl + 256], AF.Square,
                           accum_out=ssq[:, 1:2])
                    P.call("ac", "activation", ssq[:, 2:4], ssq[:, 0:2], AF.Ln,
                           scale=cscal[:, 3:4], bias=cscal[:, 0:1])
                    rms_tok[k] = P.call("ac", "activation", ssq[:, 2:4], ssq[:, 2:4],
                                        AF.Exp, scale=cscal[:, 1:2], inc=s_ac)
                    P.wait("dv", s_ac, rms_tok[k])
                    if k >= 2:
                        P.wait("dv", s_st[k % 2], st_tok[k - 2])
                    for h in range(2):
                        P.call("dv", "tensor_scalar_mul",
                               aRst[:, 256 * slot + 128 * h:256 * slot + 128 * (h + 1)],
                               pcv[:, psl + 128 * h:psl + 128 * (h + 1)],
                               ssq[:, 2 + h:3 + h],
                               inc=(s_dv if h == 1 else None))
                    ar_tok[k] = s_dv.n
                    P.wait("sy", s_dv, ar_tok[k])
                    st_tok[k] = P.dma("sy", aR_sh[128 * k:128 * (k + 1), :],
                                      aRst[:, 256 * slot:256 * (slot + 1)], s_st[k % 2])
                P.wait("po", s_st[0], s_st[0].n)
                P.wait("po", s_st[1], s_st[1].n)
                t_ag2 = P.call("po", "collective_compute", "AllGather", ALU.bypass, RG,
                               [aR_sh.opt()], [aR_all.opt()], inc=s_cc)

        P.mark("C3")
        barrier()

        with ExitStack() as pd:
            Ed = pd.enter_context
            xnr = Ed(nc.sbuf_tensor("xnr", [128, NDC * TL], BF16))
            swr = Ed(nc.sbuf_tensor("swr", [128, 4 * 6144], BF16))
            ssl2 = Ed(nc.sbuf_tensor("ssl2", [128, TL], F32))
            msh = Ed(nc.sbuf_tensor("msh", [128, 22 * TL], BF16))
            ph2 = Ed(nc.psum_tensor("ph2", [128, TL], F32))
            pg2 = Ed(nc.psum_tensor("pg2", [128, TL], F32))
            py2 = Ed(nc.psum_tensor("py2", [128, TL], F32))

            P.wait("po", s_wA, t_wA)
            P.wait("po", s_cc, t_ag1)
            t_xnr = 0
            for jj in range(NDC):
                t_xnr = P.call("po", "indirect_dma_start",
                               xnr[:, TL * jj:TL * (jj + 1)], None, xnF_all,
                               IndirectOffsetOnAxis(sxidx[:, jj:jj + 1], 0),
                               inc=s_wx, incv=16)

            SWQ = [(384 * i, min(384 * (i + 1), NSH)) for i in range(8)]
            sl_free = [0, 0, 0, 0]
            ld1 = {}

            def eload2(tag, slot, mat, si):
                P.wait("po", s_pe2, sl_free[slot])
                i0, i1 = SWQ[si]
                w = i1 - i0
                t = sw1_in if mat == "sw1" else sw3_in
                dst = swr[:, 6144 * slot:6144 * slot + NDC * w].rearrange(
                    "p (c i) -> p c i", i=w)
                ld1[tag] = (s_ws[slot], P.dma(
                    "po", dst, t[:, i0:i1].rearrange("(c p) i -> p c i", p=128),
                    s_ws[slot]))

            for si in (0, 1):
                eload2(("sw1", si), 2 * si, "sw1", si)
                eload2(("sw3", si), 2 * si + 1, "sw3", si)

            P.wait("pe", s_wx, t_xnr)
            for ic in range(22):
                si = ic // 3
                icl = ic - 3 * si
                s1, s3 = (2 * si) % 4, (2 * si + 1) % 4
                qw = SWQ[si][1] - SWQ[si][0]
                P.wait("pe", *ld1[("sw1", si)])
                P.wait("pe", *ld1[("sw3", si)])
                if ic >= 1:
                    P.wait("pe", s_ac, s_ac.n)
                    P.wait("pe", s_dv2, s_dv2.n)
                t_h2 = 0
                for (sslot, pdst, lastfl) in ((s1, ph2, False), (s3, pg2, True)):
                    for tb in range(2):
                        for jj in range(NDC):
                            P.call("pe", "matmul", pdst[:, 512 * tb:512 * (tb + 1)],
                                   swr[:, 6144 * sslot + qw * jj + 128 * icl:
                                       6144 * sslot + qw * jj + 128 * (icl + 1)],
                                   xnr[:, TL * jj + 512 * tb: TL * jj + 512 * (tb + 1)],
                                   start=(jj == 0), stop=(jj == NDC - 1),
                                   inc=(s_pe2 if (tb == 1 and jj == NDC - 1) else None))
                    if not lastfl:
                        t_h2 = s_pe2.n
                t_hg = s_pe2.n
                sl_free[s1] = max(sl_free[s1], t_hg)
                sl_free[s3] = max(sl_free[s3], t_hg)
                last_of_si = (icl == 2) or (ic == 21)
                if last_of_si and si + 2 < 8:
                    eload2(("sw1", si + 2), (2 * (si + 2)) % 4, "sw1", si + 2)
                    eload2(("sw3", si + 2), (2 * (si + 2) + 1) % 4, "sw3", si + 2)
                P.wait("ac", s_pe2, t_h2)
                t_sl = P.call("ac", "activation", ssl2[:, :], ph2[:, :], AF.Silu,
                              inc=s_ac)
                P.wait("dv", s_pe2, t_hg)
                P.wait("dv", s_ac, t_sl)
                P.call("dv", "tensor_mul", msh[:, TL * ic:TL * (ic + 1)],
                       pg2[:, :], ssl2[:, :], inc=s_dv2)
            t_msh = s_dv2.n

            W2P = [(0, 768), (768, 1536), (1536, 2304), (2304, 2816)]
            tw2 = [0] * 4
            pslots = [(0, 1), (2, 3), (0, 1), (2, 3)]

            def load_sw2(k):
                r0, r1 = W2P[k]
                a, bsl = pslots[k]
                gate = max(sl_free[a], sl_free[bsl])
                P.wait("po", s_pe2, gate)
                nch = (r1 - r0) // 128
                tw2[k] = P.dma("po",
                               swr[:, 6144 * a:6144 * a + nch * DM].rearrange(
                                   "p (c d) -> p c d", d=DM),
                               sw2_in[r0:r1, :].rearrange("(c p) d -> p c d", p=128),
                               s_ws[a])

            load_sw2(0)
            ytoks2 = []
            for k in range(4):
                a, bsl = pslots[k]
                r0, r1 = W2P[k]
                nch = (r1 - r0) // 128
                P.wait("pe", s_ws[a], tw2[k])
                if k == 0:
                    P.wait("pe", s_dv2, t_msh)
                for jd in range(NDC):
                    for tb in range(2):
                        slot = (2 * jd + tb) % 2
                        if len(ytoks2) >= 2:
                            P.wait("pe", s_dv, ytoks2[-2])
                        for icl in range(nch):
                            ic = r0 // 128 + icl
                            P.call("pe", "matmul", py2[:, 512 * slot:512 * (slot + 1)],
                                   swr[:, 6144 * a + DM * icl:6144 * a + DM * icl + DM][:, 128 * jd:128 * (jd + 1)],
                                   msh[:, TL * ic + 512 * tb: TL * ic + 512 * (tb + 1)],
                                   start=(icl == 0), stop=(icl == nch - 1),
                                   inc=(s_pe2 if icl == nch - 1 else None))
                        t_y = s_pe2.n
                        P.wait("dv", s_pe2, t_y)
                        dst = outF[:, TL * jd + 512 * tb: TL * jd + 512 * (tb + 1)]
                        P.call("dv", "tensor_add", dst, dst,
                               py2[:, 512 * slot:512 * (slot + 1)], inc=s_dv)
                        ytoks2.append(s_dv.n)
                for s_ in pslots[k]:
                    sl_free[s_] = s_pe2.n
                if k + 1 < 4:
                    load_sw2(k + 1)
            if debug_taps:
                P.wait("sy", s_dv, s_dv.n)
                P.dma("sy", taps["t_outF_d"], outF[:, :], s_stx)

        P.mark("D")
        barrier()

        with ExitStack() as pe_s:
            Ee = pe_s.enter_context
            wob = [Ee(nc.sbuf_tensor(f"wob{i}", [128, 4 * DM], BF16)) for i in range(2)]
            aRF = Ee(nc.sbuf_tensor("aRF", [128, NDC * TL], BF16))
            artok = Ee(nc.sbuf_tensor("artok", [128, 2 * CH], BF16))
            xrl = [Ee(nc.sbuf_tensor(f"xrl{i}", [128, DM], F32)) for i in range(2)]
            outst = [Ee(nc.sbuf_tensor(f"outst{i}", [128, DM], F32)) for i in range(2)]
            ptb2 = Ee(nc.psum_tensor("ptb2", [128, 512], BF16))
            ppr = Ee(nc.psum_tensor("ppr", [128, 1024], F32))
            ptf = Ee(nc.psum_tensor("ptf", [128, 512], F32))

            P.wait("po", s_cc, t_ag2)
            g_tok = [0] * NT
            tp_tok = [0] * NT
            ev_tok = [0] * NT
            for k in range(NT):
                slot = k % 2
                if k >= 2:
                    P.wait("po", s_pe2, tp_tok[k - 2])
                g_tok[k] = P.call("po", "indirect_dma_start",
                                  artok[:, CH * slot:CH * (slot + 1)], None, aR_all,
                                  IndirectOffsetOnAxis(sgidx[:, k:k + 1], 0),
                                  inc=s_wg[slot], incv=16)
                P.wait("pe", s_wg[slot], g_tok[k])
                if k >= 1:
                    P.wait("pe", s_dv2, ev_tok[k - 1])
                for half in range(2):
                    P.call("pe", "transpose", ptb2[:, 128 * half:128 * (half + 1)],
                           artok[:, CH * slot + 128 * half:CH * slot + 128 * (half + 1)],
                           idbf[:, :], inc=(s_pe2 if half == 1 else None))
                tp_tok[k] = s_pe2.n
                P.wait("dv", s_pe2, tp_tok[k])
                r, i = k // NTL, k % NTL
                for half in range(2):
                    cc2 = 2 * r + half
                    P.call("dv", "tensor_copy",
                           aRF[:, TL * cc2 + 128 * i: TL * cc2 + 128 * (i + 1)],
                           ptb2[:, 128 * half:128 * (half + 1)],
                           inc=(s_dv2 if half == 1 else None))
                ev_tok[k] = s_dv2.n
            t_aRF = s_dv2.n

            wo_use = [0, 0, 0, 0]
            wo_tok = [0, 0, 0, 0]
            wo_sc = [0, 0, 0, 0]

            def load_wo(q):
                if q >= 2:
                    P.wait("po", s_pe2, wo_use[q - 2])
                wo_tok[q] = P.dma("po",
                                  wob[q % 2][:, :].rearrange("p (c d) -> p c d", d=DM),
                                  wout_in[512 * q:512 * (q + 1), :]
                                  .rearrange("(c p) d -> p c d", p=128), s_ws[q % 2])

            load_wo(0)
            load_wo(1)
            P.wait("pe", s_dv2, t_aRF)
            for q in range(4):
                P.wait("dv", s_ws[q % 2], wo_tok[q])
                if q >= 2:
                    P.wait("dv", s_pe2, wo_use[q - 2])
                wo_sc[q] = P.call("dv", "tensor_scalar_mul", wob[q % 2][:, :],
                                  wob[q % 2][:, :], rw_col[:, 0:1], inc=s_dv2)
                P.wait("pe", s_dv2, wo_sc[q])
                if q == 0:
                    ytoks3 = []
                for jd in range(NDC):
                    for tb in range(2):
                        slot = (2 * jd + tb) % 2
                        if len(ytoks3) >= 2:
                            P.wait("pe", s_dv, ytoks3[-2])
                        for ccl in range(4):
                            cc2 = 4 * q + ccl
                            P.call("pe", "matmul", ppr[:, 512 * slot:512 * (slot + 1)],
                                   wob[q % 2][:, DM * ccl + 128 * jd: DM * ccl + 128 * (jd + 1)],
                                   aRF[:, TL * cc2 + 512 * tb: TL * cc2 + 512 * (tb + 1)],
                                   start=(ccl == 0), stop=(ccl == 3),
                                   inc=(s_pe2 if ccl == 3 else None))
                        t_y = s_pe2.n
                        P.wait("dv", s_pe2, t_y)
                        dst = outF[:, TL * jd + 512 * tb: TL * jd + 512 * (tb + 1)]
                        P.call("dv", "tensor_add", dst, dst,
                               ppr[:, 512 * slot:512 * (slot + 1)], inc=s_dv)
                        ytoks3.append(s_dv.n)
                wo_use[q] = s_pe2.n
                if q + 2 < 4:
                    load_wo(q + 2)
            t_proj = s_dv.n

            xr_tok = [0] * NTL
            fin_tok = [0] * NTL
            out_st = [0] * NTL
            P.wait("pe", s_dv, t_proj)
            for it in range(NTL):
                if it >= 2:
                    P.wait("sy", s_dv, fin_tok[it - 2])
                xr_tok[it] = P.dma("sy", xrl[it % 2][:, :],
                                   x_in[128 * it:128 * (it + 1), :], s_xa[it % 2])
                for grp in range(4):
                    if it > 0 or grp > 0:
                        P.wait("pe", s_dv, s_dv.n)
                    for j in range(4):
                        jj = 4 * grp + j
                        P.call("pe", "transpose", ptf[:, 128 * j:128 * (j + 1)],
                               outF[:, TL * jj + 128 * it: TL * jj + 128 * (it + 1)],
                               id32[:, :], inc=(s_pe2 if j == 3 else None))
                    t_tf = s_pe2.n
                    P.wait("dv", s_pe2, t_tf)
                    P.wait("dv", s_xa[it % 2], xr_tok[it])
                    if it >= 2 and grp == 0:
                        P.wait("dv", s_st[it % 2], out_st[it - 2])
                    for j in range(4):
                        jj = 4 * grp + j
                        P.call("dv", "tensor_add",
                               outst[it % 2][:, 128 * jj:128 * (jj + 1)],
                               ptf[:, 128 * j:128 * (j + 1)],
                               xrl[it % 2][:, 128 * jj:128 * (jj + 1)],
                               inc=(s_dv if j == 3 else None))
                fin_tok[it] = s_dv.n
                P.wait("sy", s_dv, fin_tok[it])
                out_st[it] = P.dma("sy", y_out[128 * it:128 * (it + 1), :],
                                   outst[it % 2][:, :], s_st[it % 2])
            P.wait("sy", s_st[0], s_st[0].n)
            P.wait("sy", s_st[1], s_st[1].n)
            P.wait("sy", s_stx, s_stx.n)

        P.mark("full")
        P.truncate(stop)

        block = E(nc.Block())

        @block.sync
        def _(sync):
            for f in P.ops["sy"]:
                f(sync)

        @block.gpsimd
        def _(g):
            for f in P.ops["po"]:
                f(g)

        @block.tensor
        def _(t):
            for f in P.ops["pe"]:
                f(t)

        @block.vector
        def _(v):
            for f in P.ops["dv"]:
                f(v)

        @block.scalar
        def _(s):
            for f in P.ops["ac"]:
                f(s)

    return nc


def _in_maps(inputs):
    consts = _make_consts()
    x = np.asarray(inputs["L0_attn_query"]).reshape(T, DM)
    wq = np.asarray(inputs["L0_attn_Wq"])
    wk = np.asarray(inputs["L0_attn_Wk"])
    wv = np.asarray(inputs["L0_attn_Wv"])
    lam = np.concatenate([
        np.asarray(inputs["L0_attn_lambda_q1"]), np.asarray(inputs["L0_attn_lambda_k1"]),
        np.asarray(inputs["L0_attn_lambda_q2"]), np.asarray(inputs["L0_attn_lambda_k2"]),
    ]).reshape(1, 256).astype(np.float32)
    rmsw = np.asarray(inputs["L0_attn_rms_weight"]).reshape(128, 1).astype(np.float32)
    shared = {
        "wout": np.ascontiguousarray(np.asarray(inputs["L0_attn_Wout"], np.float32)),
        "gate": np.ascontiguousarray(np.asarray(inputs["L0_ffn_gate_w"], np.float32)),
        "lamqk": lam,
        "rmsw": rmsw,
        "w1": np.ascontiguousarray(np.asarray(inputs["L0_ffn_w1"], np.float32)),
        "w3": np.ascontiguousarray(np.asarray(inputs["L0_ffn_w3"], np.float32)),
        "w2": np.ascontiguousarray(np.asarray(inputs["L0_ffn_w2"], np.float32)),
        "sw1": np.ascontiguousarray(np.asarray(inputs["L0_ffn_sw1"], np.float32)),
        "sw3": np.ascontiguousarray(np.asarray(inputs["L0_ffn_sw3"], np.float32)),
        "sw2": np.ascontiguousarray(np.asarray(inputs["L0_ffn_sw2"], np.float32)),
    }
    shared.update(consts)
    maps = []
    pr = np.arange(128, dtype=np.int32)
    for c in range(NCORE):
        m = dict(shared)
        m["x"] = np.ascontiguousarray(x[TL * c:TL * (c + 1), :], dtype=np.float32)
        m["wq"] = np.ascontiguousarray(wq[:, CH * c:CH * (c + 1)], dtype=np.float32)
        m["wk"] = np.ascontiguousarray(wk[:, CH * c:CH * (c + 1)], dtype=np.float32)
        m["wv"] = np.ascontiguousarray(wv[:, CH * c:CH * (c + 1)], dtype=np.float32)
        gidx = np.zeros((128, NT), np.int32)
        for k in range(NT):
            r, i = k // NTL, k % NTL
            gidx[:, k] = T * r + TL * c + 128 * i + pr
        m["gidx"] = gidx
        xidx = np.zeros((128, NDC), np.int32)
        for jj in range(NDC):
            xidx[:, jj] = DM * c + 128 * jj + pr
        m["xidx"] = xidx
        maps.append(m)
    return maps


def kernel(**inputs):
    if "nc" not in _NC_CACHE:
        _NC_CACHE["nc"] = _build(_DEBUG_TAPS)
    nc = _NC_CACHE["nc"]
    maps = _in_maps(inputs)
    res = run_bass_kernel_spmd(nc, maps, core_ids=list(range(NCORE)))
    _NC_CACHE["last_res"] = res
    y = np.concatenate([res.results[c]["y"] for c in range(NCORE)], axis=0)
    return y.reshape(1, T, DM).astype(np.float32)
